# version 01
# baseline (speedup 1.0000x reference)
import numpy as np
import concourse.bass as bass
import concourse.mybir as mybir

F32 = mybir.dt.float32
BF16 = mybir.dt.bfloat16
I32 = mybir.dt.int32
AF = mybir.ActivationFunctionType
ALU = mybir.AluOpType
AX = mybir.AxisListType

ENGINES = ("tensor", "vector", "scalar", "gpsimd", "sync")


class Op:
    __slots__ = ("engine", "fn", "reads", "writes", "is_dma", "tag", "seq",
                 "sem", "sem_val", "waits", "_deps", "semgrp")

    def __init__(self, engine, fn, reads, writes, is_dma, tag="", semgrp=None):
        self.engine = engine
        self.fn = fn
        self.reads = list(reads)
        self.writes = list(writes)
        self.is_dma = is_dma
        self.tag = tag
        self.semgrp = semgrp
        self.seq = -1
        self.sem = None
        self.sem_val = None
        self.waits = []
        self._deps = []


class Ctx:
    def __init__(self, nc):
        self.nc = nc
        self.ops = []
        self.last_writer = {}
        self.readers = {}
        self.finalized = False
        self._consts = {}

    def constf(self, val, dtype=F32):
        key = ("__const", float(val), str(dtype))
        if key not in self._consts:
            t = self.nc.alloc_sbuf_tensor(f"cst{len(self._consts)}", [128, 1], dtype)
            self.op("gpsimd", lambda e, t=t, val=val: e.memset(t[:, :], val),
                    reads=[], writes=[key], tag=f"cst{val}")
            self._consts[key] = t
        return self._consts[key], key

    def op(self, engine, fn, reads=(), writes=(), tag=""):
        o = Op(engine, fn, reads, writes, False, tag)
        self._record(o)
        return o

    def dma(self, queue, fn, reads=(), writes=(), tag="", semgrp=None):
        o = Op(queue, fn, reads, writes, True, tag, semgrp)
        self._record(o)
        return o

    def _record(self, o):
        o.seq = len(self.ops)
        deps = {}
        for k in o.reads:
            w = self.last_writer.get(k)
            if w is not None:
                deps[w.seq] = w
        for k in o.writes:
            w = self.last_writer.get(k)
            if w is not None:
                deps[w.seq] = w
            for r in self.readers.get(k, ()):
                deps[r.seq] = r
        for d in deps.values():
            if d.engine == "tensor" and o.engine == "tensor" and not d.is_dma and not o.is_dma:
                continue
            o._deps.append(d)
        for k in o.reads:
            self.readers.setdefault(k, []).append(o)
        for k in o.writes:
            self.last_writer[k] = o
            self.readers[k] = []
        self.ops.append(o)

    def finalize_and_emit(self):
        assert not self.finalized
        self.finalized = True
        nc = self.nc
        needed = set()
        for o in self.ops:
            if o.is_dma:
                needed.add(o.seq)
            for d in o._deps:
                needed.add(d.seq)
        comp_sems = {}
        dma_sems = {}
        sem_counts = {}

        def get_comp_sem(engine):
            if engine not in comp_sems:
                comp_sems[engine] = nc.alloc_semaphore(name=f"Sc_{engine}")
            return comp_sems[engine]

        def get_dma_sem(grp):
            if grp not in dma_sems:
                dma_sems[grp] = nc.alloc_semaphore(name=f"Sd_{grp}")
            return dma_sems[grp]

        for o in self.ops:
            if o.seq not in needed:
                continue
            if o.is_dma:
                if o.semgrp is not None:
                    grp = o.semgrp
                elif o.writes:
                    k = o.writes[0]
                    grp = (k if isinstance(k, str) else
                           "_".join(str(p) for p in k[:2]))
                else:
                    grp = "misc"
                o.sem = get_dma_sem(grp)
                inc = 16
            else:
                o.sem = get_comp_sem(o.engine)
                inc = 1
            key = o.sem.name
            v = sem_counts.get(key, 0) + inc
            sem_counts[key] = v
            o.sem_val = v

        know = {e: {} for e in ENGINES}
        for o in self.ops:
            k = know[o.engine]
            for d in sorted(o._deps, key=lambda d: d.seq):
                assert d.sem is not None, f"dep without sem {d.tag}->{o.tag}"
                sk = d.sem.name
                if k.get(sk, 0) >= d.sem_val:
                    continue
                o.waits.append((d.sem, d.sem_val))
                k[sk] = d.sem_val

        with nc.Block() as block:
            for ename in ENGINES:
                eops = [o for o in self.ops if o.engine == ename]
                if not eops:
                    continue

                def make_body(eops):
                    def body(eng):
                        for o in eops:
                            for (sem, val) in o.waits:
                                eng.wait_ge(sem, val)
                            ins = o.fn(eng)
                            if o.sem is not None:
                                ins.then_inc(o.sem, 16 if o.is_dma else 1)
                    return body
                getattr(block, ename)(make_body(eops))
        self.n_sems = len(comp_sems) + len(dma_sems)
        return self

    def stats(self):
        per = {}
        for o in self.ops:
            per[o.engine] = per.get(o.engine, 0) + 1
        nw = sum(len(o.waits) for o in self.ops)
        return dict(total=len(self.ops), per_engine=per, waits=nw)


"""Reusable emitters for the megakernel stages (built on klib.Ctx).

Layout conventions (per core):
 - token tiles: T tokens = nt tiles of 128 (partition dim = token).
 - feature-major transposed activations "xlnT": sbuf [128, nk*T] bf16,
   k-tile kt at cols [kt*T, (kt+1)*T); lhsT tile (kt,tt) = [:, kt*T+tt*128 ...].
 - GEMM: out[t, c] = sum_k xlnT[k,t]*W[k,c], W streamed in column slabs
   [K, G] per dma (cast fp32->bf16 in flight when needed), PSUM accumulate
   over kt, evacuated via callback.
"""
import numpy as np
import concourse.bass as bass
import concourse.mybir as mybir


def cdiv(a, b):
    return (a + b - 1) // b


class Norm:

    def __init__(self, ctx, name, x_src, T, D, eps=1e-5, rms2_eps=None,
                 nslots=2, out_dtype=BF16, xt_t=None, xln_t=None):
        nc = ctx.nc
        self.ctx, self.name, self.x_src, self.T, self.D = ctx, name, x_src, T, D
        self.eps, self.rms2_eps = eps, rms2_eps
        self.nt = T // 128
        self.nslots = nslots
        self.xt = xt_t if xt_t is not None else nc.alloc_sbuf_tensor(
            f"{name}_xt", [128, nslots * D], F32)
        self.xln = xln_t if xln_t is not None else nc.alloc_sbuf_tensor(
            f"{name}_xln", [128, nslots * D], out_dtype)
        self.st = nc.alloc_sbuf_tensor(f"{name}_stt", [128, 8 * self.nt], F32)

    def x_slot(self, tt):
        s = tt % self.nslots
        return self.xt[:, s * self.D:(s + 1) * self.D]

    def xln_slot(self, tt):
        s = tt % self.nslots
        return self.xln[:, s * self.D:(s + 1) * self.D]

    def emit_tile(self, tt, rows=128):
        ctx, name, D = self.ctx, self.name, self.D
        st = self.st
        xs = self.x_slot(tt)[:rows, :]
        sqs = self.xln_slot(tt)[:rows, :]
        slot = tt % self.nslots
        xk = (name + "x", slot)
        if self.x_src is not None:
            src_ap = self.x_src(tt)
            ctx.dma("sync", lambda e, xs=xs, src_ap=src_ap: e.dma_start(
                out=xs, in_=src_ap), reads=[], writes=[xk],
                tag=f"{name}_ldx{tt}")
        c = 8 * tt
        ctx.op("vector", lambda e, xs=xs, c=c: e.tensor_reduce(
            out=st[:rows, c:c + 1], in_=xs, axis=AX.X, op=ALU.add, negate=True),
            reads=[xk], writes=[(name, "s0", tt)], tag=f"{name}_rs{tt}")
        ctx.op("vector", lambda e, c=c: e.tensor_scalar(
            out=st[:rows, c + 1:c + 2], in0=st[:rows, c:c + 1], scalar1=1.0 / D,
            scalar2=None, op0=ALU.mult, op1=ALU.bypass),
            reads=[(name, "s0", tt)], writes=[(name, "nm", tt)], tag=f"{name}_nm{tt}")
        ctx.op("scalar", lambda e, xs=xs, sqs=sqs, c=c: e.activation(
            out=sqs, in_=xs, func=AF.Square, bias=st[:rows, c + 1:c + 2],
            scale=1.0, accum_out=st[:rows, c + 2:c + 3]),
            reads=[xk, (name, "nm", tt)], writes=[(name, "ss", tt), (name + "xln", slot)],
            tag=f"{name}_sq{tt}")
        eps_ap, eps_key = ctx.constf(self.eps)
        ctx.op("scalar", lambda e, c=c, eps_ap=eps_ap: e.activation(
            out=st[:rows, c + 3:c + 4], in_=st[:rows, c + 2:c + 3], func=AF.Ln,
            bias=eps_ap[:rows, :], scale=1.0 / D),
            reads=[(name, "ss", tt), eps_key], writes=[(name, "l1", tt)], tag=f"{name}_l1{tt}")
        ctx.op("scalar", lambda e, c=c: e.activation(
            out=st[:rows, c + 3:c + 4], in_=st[:rows, c + 3:c + 4], func=AF.Exp,
            bias=0.0, scale=-0.5),
            reads=[(name, "l1", tt)], writes=[(name, "rstd", tt)], tag=f"{name}_e1{tt}")
        scale_key = (name, "rstd", tt)
        if self.rms2_eps is not None:
            ctx.op("vector", lambda e, c=c: e.tensor_mul(
                out=st[:rows, c + 4:c + 5], in0=st[:rows, c + 3:c + 4],
                in1=st[:rows, c + 3:c + 4]),
                reads=[(name, "rstd", tt)], writes=[(name, "r2", tt)],
                tag=f"{name}_r2{tt}")
            ctx.op("vector", lambda e, c=c: e.tensor_mul(
                out=st[:rows, c + 4:c + 5], in0=st[:rows, c + 4:c + 5],
                in1=st[:rows, c + 2:c + 3]),
                reads=[(name, "r2", tt), (name, "ss", tt)],
                writes=[(name, "v2s", tt)], tag=f"{name}_v2{tt}")
            eps2_ap, eps2_key = ctx.constf(self.rms2_eps)
            ctx.op("scalar", lambda e, c=c, eps2_ap=eps2_ap: e.activation(
                out=st[:rows, c + 4:c + 5], in_=st[:rows, c + 4:c + 5], func=AF.Ln,
                bias=eps2_ap[:rows, :], scale=1.0 / D),
                reads=[(name, "v2s", tt), eps2_key], writes=[(name, "l2", tt)],
                tag=f"{name}_l2{tt}")
            ctx.op("scalar", lambda e, c=c: e.activation(
                out=st[:rows, c + 4:c + 5], in_=st[:rows, c + 4:c + 5], func=AF.Exp,
                bias=0.0, scale=-0.5),
                reads=[(name, "l2", tt)], writes=[(name, "i2", tt)],
                tag=f"{name}_e2{tt}")
            ctx.op("vector", lambda e, c=c: e.tensor_mul(
                out=st[:rows, c + 5:c + 6], in0=st[:rows, c + 4:c + 5],
                in1=st[:rows, c + 3:c + 4]),
                reads=[(name, "i2", tt), (name, "rstd", tt)],
                writes=[(name, "sc", tt)], tag=f"{name}_sc{tt}")
            scale_key = (name, "sc", tt)
            sc_col = c + 5
        else:
            sc_col = c + 3
        ctx.op("vector", lambda e, c=c, sc_col=sc_col: e.tensor_mul(
            out=st[:rows, c + 7:c + 8], in0=st[:rows, c + 1:c + 2],
            in1=st[:rows, sc_col:sc_col + 1]),
            reads=[(name, "nm", tt), scale_key], writes=[(name, "b2", tt)],
            tag=f"{name}_b2{tt}")
        xo = self.xln_slot(tt)[:rows, :]
        ctx.op("scalar", lambda e, xs=xs, xo=xo, c=c, sc_col=sc_col: e.activation(
            out=xo, in_=xs, func=AF.Identity, bias=st[:rows, c + 7:c + 8],
            scale=st[:rows, sc_col:sc_col + 1]),
            reads=[xk, (name, "b2", tt), scale_key],
            writes=[(name + "xln", slot)], tag=f"{name}_ap{tt}")
        return (name + "xln", slot)


def emit_transpose_tiles(ctx, name, src_tile_fn, src_keys, dst, dst_key_name,
                         nk, tt, T, ident, psA, psB, evac_engine="scalar",
                         dtype=BF16):
    for kg in range(cdiv(nk, 4)):
        ps = psA if kg % 2 == 0 else psB
        psk = (name + "_tps", tt, kg % 2)
        kts = range(kg * 4, min(nk, kg * 4 + 4))
        for i, kt in enumerate(kts):
            ctx.op("tensor", lambda e, i=i, kt=kt, ps=ps: e.matmul(
                ps[:, i * 128:(i + 1) * 128], src_tile_fn(kt), ident[:, :],
                start=True, stop=True),
                reads=[src_keys(kt), "ident"], writes=[psk],
                tag=f"{name}_tp{tt}_{kt}")
        n = len(kts) * 128

        def ev(e, ps=ps, kts=kts, n=n):
            pass
        for i, kt in enumerate(kts):
            ctx.op(evac_engine, lambda e, i=i, kt=kt, ps=ps: (
                e.activation(out=dst[:, kt * T + tt * 128: kt * T + tt * 128 + 128],
                             in_=ps[:, i * 128:(i + 1) * 128], func=AF.Copy)
                if evac_engine == "scalar" else
                e.tensor_copy(out=dst[:, kt * T + tt * 128: kt * T + tt * 128 + 128],
                              in_=ps[:, i * 128:(i + 1) * 128])),
                reads=[psk], writes=[(dst_key_name, kt, tt)],
                tag=f"{name}_te{tt}_{kt}")


class Gemm:

    def __init__(self, ctx, name, nbanks=4, slabG=512, nslots=3):
        nc = ctx.nc
        self.ctx, self.name = ctx, name
        self.slabG, self.nslots = slabG, nslots
        self.banks = [nc.alloc_psum_tensor(f"{name}_pb{i}", [128, 512], F32)
                      for i in range(nbanks)]
        self.slab = None
        self._bank_i = 0

    def run(self, xlnT, xlnT_keyname, xlnT_keys, nt, nk, T, w_src, wf32, cols,
            out_cb, w_slab_name=None, tiles=None):
        ctx, name = self.ctx, self.name
        nc = ctx.nc
        G = self.slabG
        if self.slab is None:
            self.slab = nc.alloc_sbuf_tensor(
                f"{name}_wsl", [128, self.nslots * nk * G], BF16)
        wname = w_slab_name or (name + "_wsl")
        ncg = cdiv(cols, G)
        tiles = list(range(nt)) if tiles is None else list(tiles)
        for cg in range(ncg):
            c0 = cg * G
            g = min(G, cols - c0)
            slot = cg % self.nslots
            sl = self.slab[:, slot * nk * G: slot * nk * G + nk * g]
            q = "gpsimd" if wf32 else "sync"
            sl3 = sl.rearrange("p (kt c) -> p kt c", c=g)
            ctx.dma(q, lambda e, sl3=sl3, c0=c0, g=g: e.dma_start(
                out=sl3, in_=w_src(c0, g)), reads=[], writes=[(wname, slot)],
                tag=f"{name}_ldw{cg}")
            for tt in tiles:
                nb = cdiv(g, 512)
                for b in range(nb):
                    w = min(512, g - b * 512)
                    bk = self.banks[self._bank_i % len(self.banks)]
                    psk = (name + "_ps", self._bank_i % len(self.banks))
                    self._bank_i += 1
                    for kt in range(nk):
                        ctx.op("tensor", lambda e, kt=kt, tt=tt, b=b, w=w, bk=bk, sl=sl, g=g: e.matmul(
                            bk[:, :w],
                            xlnT[:, kt * T + tt * 128: kt * T + tt * 128 + 128],
                            sl[:, kt * g + b * 512: kt * g + b * 512 + w],
                            start=(kt == 0), stop=(kt == nk - 1)),
                            reads=[(wname, slot), xlnT_keys(kt, tt)],
                            writes=[psk], tag=f"{name}_mm{cg}_{tt}_{b}_{kt}")
                    out_cb(ctx, tt, c0 + b * 512, w, bk, psk)


def dram_colslab(w_dram, row0, K, c0, G, ncols_total=None):
    return w_dram[row0:row0 + K, c0:c0 + G].rearrange("(kt p) c -> p kt c", p=128)


"""Module-specific stage emitters for the mk_seq_preln_d2 megakernel.

Per-core token shard: T=1024 rows of the 8192-token sequence.
PSUM convention: each psum tensor has ONE tracker key = its name.
"""
import numpy as np
import concourse.bass as bass
import concourse.mybir as mybir


def sap(t, F, off, dims):
    return bass.AP(t, off, [[F, 128]] + dims)


def emit_ident(ctx, id_dram):
    nc = ctx.nc
    ident = nc.alloc_sbuf_tensor("ident", [128, 128], BF16)
    ctx.dma("gpsimd", lambda e: e.dma_start(out=ident[:, :], in_=id_dram[:, :]),
            reads=[], writes=["ident"], tag="ldid")
    return ident


def emit_xlnT(ctx, name, norm, xlnT, xlnT_F, nk, T, ident, psA, psB,
              evac="vector"):
    nt = T // 128
    for tt in range(nt):
        norm.emit_tile(tt)
        xlk = (norm.name + "xln", tt % norm.nslots)
        for kg in range(cdiv(nk, 8)):
            ps, psk = (psA, (name + "psA",)) if kg % 2 == 0 else (psB, (name + "psB",))
            kts = list(range(kg * 8, min(nk, kg * 8 + 8)))
            for i, kt in enumerate(kts):
                ctx.op("tensor", lambda e, i=i, kt=kt, ps=ps, tt=tt: e.matmul(
                    ps[:, i * 128:(i + 1) * 128],
                    norm.xln_slot(tt)[:, kt * 128:(kt + 1) * 128], ident[:, :],
                    start=True, stop=True),
                    reads=[xlk, "ident"], writes=[psk], tag=f"{name}tp{tt}_{kt}")
            n = len(kts)
            dst = sap(xlnT, xlnT_F, kts[0] * T + tt * 128, [[T, n], [1, 128]])
            wk = [("xlnT" + name, kt, tt) for kt in kts]
            if evac == "vector":
                ctx.op("vector", lambda e, ps=ps, dst=dst, n=n: e.tensor_copy(
                    out=dst, in_=ps[:, :n * 128]),
                    reads=[psk], writes=wk, tag=f"{name}te{tt}_{kg}")
            else:
                ctx.op("scalar", lambda e, ps=ps, dst=dst, n=n: e.activation(
                    out=dst, in_=ps[:, :n * 128], func=AF.Copy),
                    reads=[psk], writes=wk, tag=f"{name}te{tt}_{kg}")



def emit_dequant_slice(ctx, name, q_dram, s_dram, z_dram, out_dram,
                       OUTR, PACK, ident, CH=8, store_q="sync", ps=None,
                       pskn=None):
    nc = ctx.nc
    NG = PACK // 32
    IN = 2 * PACK
    ntile = OUTR // 128
    nch = cdiv(ntile, CH)
    Fq = 2 * CH * PACK
    qt = nc.alloc_sbuf_tensor(f"{name}_q", [128, Fq], I32)
    hi = nc.alloc_sbuf_tensor(f"{name}_hi", [128, CH * PACK], I32)
    lo = nc.alloc_sbuf_tensor(f"{name}_lo", [128, CH * PACK], I32)
    Fs = 2 * CH * NG
    sc = nc.alloc_sbuf_tensor(f"{name}_sc", [128, Fs], F32)
    zsb_t = nc.alloc_sbuf_tensor(f"{name}_zs", [128, CH * NG], F32)
    Fdq = CH * IN
    dq = nc.alloc_sbuf_tensor(f"{name}_dq", [128, Fdq], BF16)
    if ps is None:
        ps = [nc.alloc_psum_tensor(f"{name}_ps{i}", [128, 1024], F32)
              for i in range(2)]
        pskn = name
    stg = nc.alloc_sbuf_tensor(f"{name}_stg", [128, 2 * 1024], BF16)

    for ch in range(nch):
        r0 = ch * CH
        nrt = min(CH, ntile - r0)
        slot = ch % 2
        qs = qt[:, slot * CH * PACK: slot * CH * PACK + nrt * PACK]
        ctx.dma(store_q, lambda e, qs=qs, r0=r0, nrt=nrt: e.dma_start(
            out=qs.rearrange("p (a c) -> p a c", c=PACK),
            in_=q_dram[r0 * 128:(r0 + nrt) * 128, :].rearrange(
                "(a p) c -> p a c", p=128)),
            reads=[], writes=[(name + "q", slot)], tag=f"{name}_ldq{ch}")
        scs = sc[:, slot * CH * NG: slot * CH * NG + nrt * NG]
        ctx.dma(store_q, lambda e, scs=scs, r0=r0, nrt=nrt: e.dma_start(
            out=scs.rearrange("p (a c) -> p a c", c=NG),
            in_=s_dram[r0 * 128:(r0 + nrt) * 128, :].rearrange(
                "(a p) c -> p a c", p=128)),
            reads=[], writes=[(name + "s", slot)], tag=f"{name}_lds{ch}")
        zss = zsb_t[:, :nrt * NG]
        ctx.dma(store_q, lambda e, zss=zss, r0=r0, nrt=nrt: e.dma_start(
            out=zss.rearrange("p (a c) -> p a c", c=NG),
            in_=z_dram[r0 * 128:(r0 + nrt) * 128, :].rearrange(
                "(a p) c -> p a c", p=128)),
            reads=[], writes=[(name + "z", 0)], tag=f"{name}_ldz{ch}")
        ctx.op("vector", lambda e, zss=zss, scs=scs: e.tensor_mul(
            out=zss, in0=zss, in1=scs),
            reads=[(name + "z", 0), (name + "s", slot)],
            writes=[(name + "z", 0)], tag=f"{name}_zs{ch}")
        his = hi[:, :nrt * PACK]
        los = lo[:, :nrt * PACK]
        ctx.op("vector", lambda e, his=his, qs=qs: e.tensor_scalar(
            out=his, in0=qs, scalar1=4, scalar2=15,
            op0=ALU.logical_shift_right, op1=ALU.bitwise_and),
            reads=[(name + "q", slot)], writes=[(name + "hi", 0)],
            tag=f"{name}_hi{ch}")
        ctx.op("vector", lambda e, los=los, qs=qs: e.tensor_scalar(
            out=los, in0=qs, scalar1=15, scalar2=None,
            op0=ALU.bitwise_and, op1=ALU.bypass),
            reads=[(name + "q", slot)], writes=[(name + "lo", 0)],
            tag=f"{name}_lo{ch}")
        scb = sap(sc, Fs, slot * CH * NG, [[NG, nrt], [1, NG], [0, 32]])
        zsb = sap(zsb_t, CH * NG, 0, [[NG, nrt], [1, NG], [0, 32]])
        hi3 = his.rearrange("p (a c) -> p a c", c=PACK)
        lo3 = los.rearrange("p (a c) -> p a c", c=PACK)
        dq_hi = sap(dq, Fdq, 0, [[IN, nrt], [2, PACK]])
        dq_lo = sap(dq, Fdq, 1, [[IN, nrt], [2, PACK]])
        for (nib3, dqx, kk) in ((hi3, dq_hi, "hi"), (lo3, dq_lo, "lo")):
            ctx.op("vector", lambda e, nib3=nib3, dqx=dqx, scb=scb: e.tensor_mul(
                out=dqx, in0=nib3, in1=scb),
                reads=[(name + kk, 0), (name + "s", slot)],
                writes=[(name + "dq", kk)], tag=f"{name}_m{kk}{ch}")
            ctx.op("vector", lambda e, dqx=dqx, zsb=zsb: e.tensor_sub(
                out=dqx, in0=dqx, in1=zsb),
                reads=[(name + "dq", kk), (name + "z", 0)],
                writes=[(name + "dq", kk)], tag=f"{name}_s{kk}{ch}")
        nct = IN // 128
        for a in range(nrt):
            for cg in range(cdiv(nct, 8)):
                p = ps[cg % 2]
                psk = (pskn + "ps", cg % 2)
                cts = list(range(cg * 8, min(nct, cg * 8 + 8)))
                n = len(cts)
                for i, ct in enumerate(cts):
                    src = dq[:, a * IN + ct * 128: a * IN + (ct + 1) * 128]
                    ctx.op("tensor", lambda e, i=i, src=src, p=p: e.matmul(
                        p[:, i * 128:(i + 1) * 128], src, ident[:, :],
                        start=True, stop=True),
                        reads=[(name + "dq", "hi"), (name + "dq", "lo"), "ident"],
                        writes=[psk], tag=f"{name}_tp{ch}_{a}_{ct}")
                st = stg[:, (cg % 2) * 1024:(cg % 2) * 1024 + n * 128]
                ctx.op("vector", lambda e, p=p, st=st, n=n: e.tensor_copy(
                    out=st, in_=p[:, :n * 128]),
                    reads=[psk], writes=[(name + "stg", cg % 2)],
                    tag=f"{name}_tv{ch}_{a}_{cg}")
                dst = out_dram[cts[0] * 128:(cts[0] + n) * 128,
                               (r0 + a) * 128:(r0 + a + 1) * 128].rearrange(
                    "(n p) c -> p n c", p=128)
                st3 = st.rearrange("p (n c) -> p n c", c=128)
                ctx.dma("scalar", lambda e, dst=dst, st3=st3: e.dma_start(
                    out=dst, in_=st3),
                    reads=[(name + "stg", cg % 2)],
                    writes=[(name + "out", cg % 2)], tag=f"{name}_st{ch}_{a}_{cg}")
    return [(name + "out", 0), (name + "out", 1)]



def emit_l0_ffn(ctx, x1_dram, pg_dram, pu_dram, pd_dram, x2_dram, ident,
                T=1024, D=4096, I=11008, name="f0", passes=None):
    nc = ctx.nc
    nt, nk = T // 128, D // 128
    n_it = I // 128
    if passes is None:
        passes = [22, 22, 21, 21] if n_it == 86 else [n_it]
    assert sum(passes) == n_it
    maxp = max(passes)

    FxT = nk * T
    big = (T == 1024 and D == 4096)
    SLOT = nk * 512
    if big:
        O = 20448
        xlnT = nc.alloc_sbuf_tensor_at(f"{name}_xlnT", [128, FxT], BF16, offset=O)
        wsl = nc.alloc_sbuf_tensor_at(f"{name}_wsl", [128, 2 * SLOT], BF16,
                                      offset=O + 65536)
        gT = nc.alloc_sbuf_tensor_at(f"{name}_gT", [128, 4 * T], F32,
                                     offset=O + 131072)
        stg = nc.alloc_sbuf_tensor_at(f"{name}_stg", [128, 3 * 512], F32,
                                      offset=O + 147456)
        xr = nc.alloc_sbuf_tensor_at(f"{name}_xr", [128, 3 * 512], F32,
                                     offset=O + 153600)
        xt_t = nc.alloc_sbuf_tensor_at(f"{name}_xt", [128, 2 * D], F32,
                                       offset=O + 159744)
        xln_t = nc.alloc_sbuf_tensor_at(f"{name}_xln", [128, 2 * D], BF16,
                                        offset=O + 192512)
        hT = nc.alloc_sbuf_tensor_at(f"{name}_hT", [128, maxp * T], BF16,
                                     offset=O + 159744)
    else:
        xlnT = nc.alloc_sbuf_tensor(f"{name}_xlnT", [128, FxT], BF16)
        wsl = nc.alloc_sbuf_tensor(f"{name}_wsl", [128, 2 * SLOT], BF16)
        gT = nc.alloc_sbuf_tensor(f"{name}_gT", [128, 4 * T], F32)
        stg = nc.alloc_sbuf_tensor(f"{name}_stg", [128, 3 * 512], F32)
        xr = nc.alloc_sbuf_tensor(f"{name}_xr", [128, 3 * 512], F32)
        xt_t = xln_t = None
        hT = nc.alloc_sbuf_tensor(f"{name}_hT", [128, maxp * T], BF16)
    psA = nc.alloc_psum_tensor(f"{name}_psA", [128, 1024], F32)
    psB = nc.alloc_psum_tensor(f"{name}_psB", [128, 1024], F32)
    psC = nc.alloc_psum_tensor(f"{name}_psC", [128, 1024], F32)
    psD = nc.alloc_psum_tensor(f"{name}_psD", [128, 1024], F32)
    KA, KB, KC, KD = (name + "psA",), (name + "psB",), (name + "psC",), (name + "psD",)

    norm = Norm(ctx, name + "n", lambda tt: x1_dram[tt * 128:(tt + 1) * 128, :],
                T, D, nslots=2, xt_t=xt_t, xln_t=xln_t)
    emit_xlnT(ctx, name, norm, xlnT, FxT, nk, T, ident, psA, psB)
    xlnT_keys_all = [("xlnT" + name, kt, tt) for kt in range(nk)
                     for tt in range(nt)]
    if big:
        ctx.op("gpsimd", lambda e: e.engine_nop(), reads=xlnT_keys_all,
               writes=[(name + "hT", itl) for itl in range(maxp)],
               tag=f"{name}_fence")

    wcount = [0]
    npass = len(passes)
    for p, nkt in enumerate(passes):
        it0 = sum(passes[:p])
        for cgl in range(cdiv(nkt, 4)):
            its = list(range(cgl * 4, min(nkt, cgl * 4 + 4)))
            gw = len(its) * 128
            icol0 = (it0 + its[0]) * 128
            for gi, wd in ((0, pg_dram), (1, pu_dram)):
                slot = wcount[0] % 2
                wcount[0] += 1
                sl = wsl[:, slot * SLOT: slot * SLOT + nk * gw]
                src = wd[:, icol0:icol0 + gw].rearrange("(kt p) c -> p kt c", p=128)
                ctx.dma("sync", lambda e, sl=sl, src=src, gw=gw: e.dma_start(
                    out=sl.rearrange("p (kt c) -> p kt c", c=gw), in_=src),
                    reads=[], writes=[(name + "wsl", slot)],
                    tag=f"{name}_ld{'gu'[gi]}{p}_{cgl}")
                for ii, itl in enumerate(its):
                    if gi == 0:
                        pp, psk = (psA, KA) if itl % 2 == 0 else (psB, KB)
                    else:
                        pp, psk = (psC, KC) if itl % 2 == 0 else (psD, KD)
                    for kt in range(nk):
                        for h in range(cdiv(T, 512)):
                            hw_ = min(512, T - h * 512)
                            ctx.op("tensor", lambda e, pp=pp, kt=kt, h=h, sl=sl, ii=ii, gw=gw, hw_=hw_: e.matmul(
                                pp[:, h * 512: h * 512 + hw_],
                                sl[:, kt * gw + ii * 128: kt * gw + ii * 128 + 128],
                                xlnT[:, kt * T + h * 512: kt * T + h * 512 + hw_],
                                start=(kt == 0), stop=(kt == nk - 1)),
                                reads=[(name + "wsl", slot)] + [
                                    ("xlnT" + name, kt, h * 4 + j)
                                    for j in range(hw_ // 128)],
                                writes=[psk],
                                tag=f"{name}_{'gu'[gi]}mm{p}_{cgl}_{ii}_{kt}_{h}")
                    if gi == 0:
                        g_ap = gT[:, (itl % 4) * T:(itl % 4) * T + T]
                        ctx.op("scalar", lambda e, pp=pp, g_ap=g_ap: e.activation(
                            out=g_ap, in_=pp[:, :T], func=AF.Silu),
                            reads=[psk], writes=[(name + "gT", itl % 4)],
                            tag=f"{name}_gev{p}_{cgl}_{ii}")
                    else:
                        g_ap = gT[:, (itl % 4) * T:(itl % 4) * T + T]
                        hdst = hT[:, itl * T:(itl + 1) * T]
                        ctx.op("vector", lambda e, pp=pp, g_ap=g_ap, hdst=hdst: e.tensor_mul(
                            out=hdst, in0=pp[:, :T], in1=g_ap),
                            reads=[psk, (name + "gT", itl % 4)],
                            writes=[(name + "hT", itl)],
                            tag=f"{name}_hev{p}_{cgl}_{ii}")
        for jc in range(D // 512):
            slot = wcount[0] % 2
            wcount[0] += 1
            sl = wsl[:, slot * SLOT: slot * SLOT + nkt * 512]
            src = pd_dram[it0 * 128:(it0 + nkt) * 128,
                          jc * 512:(jc + 1) * 512].rearrange(
                "(kt p) c -> p kt c", p=128)
            ctx.dma("sync", lambda e, sl=sl, src=src: e.dma_start(
                out=sl.rearrange("p (kt c) -> p kt c", c=512), in_=src),
                reads=[], writes=[(name + "wsl", slot)],
                tag=f"{name}_ldd{p}_{jc}")
            for tt in range(nt):
                pp, psk = (psA, KA) if tt % 2 == 0 else (psB, KB)
                for kt in range(nkt):
                    ctx.op("tensor", lambda e, pp=pp, kt=kt, tt=tt, sl=sl: e.matmul(
                        pp[:, :512],
                        hT[:, kt * T + tt * 128: kt * T + tt * 128 + 128],
                        sl[:, kt * 512:(kt + 1) * 512],
                        start=(kt == 0), stop=(kt == nkt - 1)),
                        reads=[(name + "wsl", slot), (name + "hT", kt)],
                        writes=[psk], tag=f"{name}_dmm{p}_{jc}_{tt}_{kt}")
                si = (jc * nt + tt) % 3
                ss = stg[:, si * 512:(si + 1) * 512]
                if p < npass - 1:
                    ctx.op("scalar", lambda e, pp=pp, ss=ss: e.activation(
                        out=ss, in_=pp[:, :512], func=AF.Copy),
                        reads=[psk], writes=[(name + "stg", si)],
                        tag=f"{name}_dev{p}_{jc}_{tt}")
                else:
                    xrs = xr[:, si * 512:(si + 1) * 512]
                    ctx.dma("scalar", lambda e, xrs=xrs, tt=tt, jc=jc: e.dma_start(
                        out=xrs, in_=x1_dram[tt * 128:(tt + 1) * 128,
                                             jc * 512:(jc + 1) * 512]),
                        reads=[], writes=[(name + "xr", si)],
                        tag=f"{name}_ldxr{jc}_{tt}")
                    ctx.op("vector", lambda e, pp=pp, ss=ss, xrs=xrs: e.tensor_add(
                        out=ss, in0=pp[:, :512], in1=xrs),
                        reads=[psk, (name + "xr", si)], writes=[(name + "stg", si)],
                        tag=f"{name}_dev{p}_{jc}_{tt}")
                stslot = (jc * nt + tt) % 8
                accum = mybir.AluOpType.add if p > 0 else mybir.AluOpType.bypass
                ctx.dma("gpsimd", lambda e, ss=ss, tt=tt, jc=jc, accum=accum: e.dma_start(
                    out=x2_dram[tt * 128:(tt + 1) * 128, jc * 512:(jc + 1) * 512],
                    in_=ss, accum_op=accum),
                    reads=[(name + "stg", si)], writes=[(name + "x2st", stslot)],
                    tag=f"{name}_stx2{p}_{jc}_{tt}")
    return [(name + "x2st", i) for i in range(8)]



def emit_l1_ffn(ctx, x3_dram, wbT_dram, wgT_dram, aT_dram, bstk_dram, bb_dram,
                ones_dram, x4_dram, ident, T=1024, D=4096, name="m0",
                E=8, R=16):
    nc = ctx.nc
    nt, nk = T // 128, D // 128
    ER = E * R

    FxT = nk * T
    xlnT = nc.alloc_sbuf_tensor(f"{name}_xlnT", [128, FxT], BF16)
    SLOT = nk * 256
    wsl = nc.alloc_sbuf_tensor(f"{name}_wsl", [128, 2 * SLOT], BF16)
    psA = nc.alloc_psum_tensor(f"{name}_psA", [128, 1024], F32)
    psB = nc.alloc_psum_tensor(f"{name}_psB", [128, 1024], F32)
    psC = nc.alloc_psum_tensor(f"{name}_psC", [128, 512], F32)
    KA, KB, KC = (name + "psA",), (name + "psB",), (name + "psC",)
    at = nc.alloc_sbuf_tensor(f"{name}_at", [128, nk * ER], BF16)
    bstk = nc.alloc_sbuf_tensor(f"{name}_bs", [128, D], BF16)
    wg = nc.alloc_sbuf_tensor(f"{name}_wg", [128, nk * E], BF16)
    bb = nc.alloc_sbuf_tensor(f"{name}_bb", [1, D], BF16)
    ones = nc.alloc_sbuf_tensor(f"{name}_on", [1, 128], BF16)
    lg = nc.alloc_sbuf_tensor(f"{name}_lg", [128, nt * E], F32)
    rt = nc.alloc_sbuf_tensor(f"{name}_rt", [128, nt * E], F32)
    t1b = nc.alloc_sbuf_tensor(f"{name}_t1", [128, 2 * E], F32)
    sstat = nc.alloc_sbuf_tensor(f"{name}_ss", [128, 4 * nt], F32)
    hall = nc.alloc_sbuf_tensor(f"{name}_ha", [128, nt * ER], F32)
    hwb = nc.alloc_sbuf_tensor(f"{name}_hw", [128, 2 * ER], BF16)
    hwT = nc.alloc_sbuf_tensor(f"{name}_hT", [128, nt * 128], BF16)
    stg = nc.alloc_sbuf_tensor(f"{name}_stg", [128, 3 * 512], F32)
    xr = nc.alloc_sbuf_tensor(f"{name}_xr", [128, 3 * 512], F32)

    for t_dram, t_sb, kk, w in ((aT_dram, at, "at", ER), (wgT_dram, wg, "wg", E),
                                (bstk_dram, bstk, "bs", None),
                                (bb_dram, bb, "bb", None),
                                (ones_dram, ones, "on", None)):
        if w is None:
            ctx.dma("gpsimd", lambda e, t_sb=t_sb, t_dram=t_dram: e.dma_start(
                out=t_sb[:, :], in_=t_dram[:, :]), reads=[],
                writes=[(name + kk, 0)], tag=f"{name}_ld{kk}")
        else:
            src2 = t_dram[:, :].rearrange("(kt p) c -> p kt c", p=128)
            ctx.dma("gpsimd", lambda e, t_sb=t_sb, src2=src2, w=w: e.dma_start(
                out=t_sb[:, :].rearrange("p (kt c) -> p kt c", c=w),
                in_=src2), reads=[], writes=[(name + kk, 0)], tag=f"{name}_ld{kk}")

    norm = Norm(ctx, name + "n", lambda tt: x3_dram[tt * 128:(tt + 1) * 128, :],
                T, D, nslots=2)
    emit_xlnT(ctx, name, norm, xlnT, FxT, nk, T, ident, psA, psB)

    for tt in range(nt):
        for kt in range(nk):
            ctx.op("tensor", lambda e, kt=kt, tt=tt: e.matmul(
                psC[:, :E], xlnT[:, kt * T + tt * 128: kt * T + tt * 128 + 128],
                wg[:, kt * E:(kt + 1) * E], start=(kt == 0), stop=(kt == nk - 1)),
                reads=[(name + "wg", 0), ("xlnT" + name, kt, tt)], writes=[KC],
                tag=f"{name}_lg{tt}_{kt}")
        lgs = lg[:, tt * E:(tt + 1) * E]
        ctx.op("vector", lambda e, lgs=lgs: e.tensor_copy(out=lgs, in_=psC[:, :E]),
               reads=[KC], writes=[(name + "lg", tt)], tag=f"{name}_lgc{tt}")
        for kt in range(nk):
            ctx.op("tensor", lambda e, kt=kt, tt=tt: e.matmul(
                psC[:, :ER], xlnT[:, kt * T + tt * 128: kt * T + tt * 128 + 128],
                at[:, kt * ER:(kt + 1) * ER], start=(kt == 0), stop=(kt == nk - 1)),
                reads=[(name + "at", 0), ("xlnT" + name, kt, tt), (name + "lg", tt)],
                writes=[KC], tag=f"{name}_ha{tt}_{kt}")
        has = hall[:, tt * ER:(tt + 1) * ER]
        ctx.op("vector", lambda e, has=has: e.tensor_copy(out=has, in_=psC[:, :ER]),
               reads=[KC], writes=[(name + "ha", tt)], tag=f"{name}_hac{tt}")
        c = 4 * tt
        st_ = sstat
        ctx.op("vector", lambda e, lgs=lgs, c=c: e.tensor_reduce(
            out=st_[:, c:c + 1], in_=lgs, axis=AX.X, op=ALU.max, negate=True),
            reads=[(name + "lg", tt)], writes=[(name, "mx", tt)],
            tag=f"{name}_mx{tt}")
        es = rt[:, tt * E:(tt + 1) * E]
        ctx.op("scalar", lambda e, es=es, lgs=lgs, c=c: e.activation(
            out=es, in_=lgs, func=AF.Exp, bias=st_[:, c:c + 1], scale=1.0),
            reads=[(name + "lg", tt), (name, "mx", tt)], writes=[(name, "e", tt)],
            tag=f"{name}_ex{tt}")
        t1 = t1b[:, (tt % 2) * E:(tt % 2 + 1) * E]
        ctx.op("vector", lambda e, es=es, t1=t1: e.tensor_scalar(
            out=t1, in0=es, scalar1=1.0, scalar2=None, op0=ALU.is_ge,
            op1=ALU.bypass),
            reads=[(name, "e", tt)], writes=[(name + "t1", tt % 2)],
            tag=f"{name}_i1{tt}")
        ctx.op("vector", lambda e, es=es, t1=t1: e.tensor_sub(
            out=t1, in0=es, in1=t1),
            reads=[(name, "e", tt), (name + "t1", tt % 2)],
            writes=[(name + "t1", tt % 2)], tag=f"{name}_e2{tt}")
        ctx.op("vector", lambda e, t1=t1, c=c: e.tensor_reduce(
            out=st_[:, c + 1:c + 2], in_=t1, axis=AX.X, op=ALU.max),
            reads=[(name + "t1", tt % 2)], writes=[(name, "m2", tt)],
            tag=f"{name}_m2{tt}")
        ctx.op("vector", lambda e, es=es, t1=t1, c=c: e.tensor_scalar(
            out=t1, in0=es, scalar1=st_[:, c + 1:c + 2], scalar2=None,
            op0=ALU.is_ge, op1=ALU.bypass),
            reads=[(name, "e", tt), (name, "m2", tt)],
            writes=[(name + "t1", tt % 2)], tag=f"{name}_i2{tt}")
        ctx.op("vector", lambda e, c=c: e.tensor_scalar(
            out=st_[:, c + 2:c + 3], in0=st_[:, c + 1:c + 2], scalar1=1.0,
            scalar2=None, op0=ALU.add, op1=ALU.bypass),
            reads=[(name, "m2", tt)], writes=[(name, "dn", tt)],
            tag=f"{name}_dn{tt}")
        ctx.op("vector", lambda e, c=c: e.reciprocal(
            out=st_[:, c + 3:c + 4], in_=st_[:, c + 2:c + 3]),
            reads=[(name, "dn", tt)], writes=[(name, "rc", tt)],
            tag=f"{name}_rc{tt}")
        ctx.op("vector", lambda e, es=es, t1=t1: e.tensor_mul(
            out=es, in0=es, in1=t1),
            reads=[(name, "e", tt), (name + "t1", tt % 2)],
            writes=[(name, "e", tt)], tag=f"{name}_rm{tt}")
        ctx.op("vector", lambda e, es=es, c=c: e.tensor_scalar(
            out=es, in0=es, scalar1=st_[:, c + 3:c + 4], scalar2=None,
            op0=ALU.mult, op1=ALU.bypass),
            reads=[(name, "e", tt), (name, "rc", tt)], writes=[(name + "rtf", tt)],
            tag=f"{name}_rw{tt}")
        rbc = sap(rt, nt * E, tt * E, [[1, E], [0, R]])
        hws = hwb[:, (tt % 2) * ER:(tt % 2 + 1) * ER]
        ctx.op("vector", lambda e, has=has, rbc=rbc, hws=hws: e.tensor_mul(
            out=hws, in0=has, in1=rbc),
            reads=[(name + "rtf", tt), (name + "ha", tt)],
            writes=[(name + "hwb", tt % 2)], tag=f"{name}_hm{tt}")
        ctx.op("tensor", lambda e, hws=hws: e.matmul(
            psC[:, :ER], hws, ident[:, :ER], start=True, stop=True),
            reads=[(name + "hwb", tt % 2), "ident"], writes=[KC],
            tag=f"{name}_ht{tt}")
        ctx.op("vector", lambda e, tt=tt: e.tensor_copy(
            out=hwT[:ER, tt * 128: tt * 128 + 128],
            in_=psC[:ER, :128]),
            reads=[KC], writes=[(name + "hwT", tt)], tag=f"{name}_hte{tt}")

    CGW = 256
    for cg in range(D // CGW):
        slot = cg % 2
        sl = wsl[:, slot * SLOT: slot * SLOT + nk * CGW]
        src2 = wbT_dram[:, cg * CGW:(cg + 1) * CGW].rearrange(
            "(kt p) c -> p kt c", p=128)
        ctx.dma("gpsimd", lambda e, sl=sl, src2=src2: e.dma_start(
            out=sl.rearrange("p (kt c) -> p kt c", c=CGW), in_=src2),
            reads=[], writes=[(name + "wsl", slot)], tag=f"{name}_ldw{cg}")
        for tt in range(nt):
            pp, psk = (psA, KA) if tt % 2 == 0 else (psB, KB)
            for kt in range(nk):
                ctx.op("tensor", lambda e, pp=pp, kt=kt, tt=tt, sl=sl: e.matmul(
                    pp[:, :CGW],
                    xlnT[:, kt * T + tt * 128: kt * T + tt * 128 + 128],
                    sl[:, kt * CGW:(kt + 1) * CGW],
                    start=(kt == 0), stop=False),
                    reads=[(name + "wsl", slot), ("xlnT" + name, kt, tt)],
                    writes=[psk], tag=f"{name}_bmm{cg}_{tt}_{kt}")
            ctx.op("tensor", lambda e, pp=pp, cg=cg: e.matmul(
                pp[:, :CGW], ones[:1, :], bb[:1, cg * CGW:(cg + 1) * CGW],
                start=False, stop=False),
                reads=[(name + "on", 0), (name + "bb", 0)], writes=[psk],
                tag=f"{name}_bias{cg}_{tt}")
            ctx.op("tensor", lambda e, pp=pp, cg=cg, tt=tt: e.matmul(
                pp[:, :CGW], hwT[:ER, tt * 128: tt * 128 + 128],
                bstk[:ER, cg * CGW:(cg + 1) * CGW],
                start=False, stop=True),
                reads=[(name + "bs", 0), (name + "hwT", tt)], writes=[psk],
                tag=f"{name}_dmm{cg}_{tt}")
            si = (cg * nt + tt) % 3
            ss = stg[:, si * 512:si * 512 + CGW]
            xrs = xr[:, si * 512:si * 512 + CGW]
            ctx.dma("scalar", lambda e, xrs=xrs, tt=tt, cg=cg: e.dma_start(
                out=xrs, in_=x3_dram[tt * 128:(tt + 1) * 128,
                                     cg * CGW:(cg + 1) * CGW]),
                reads=[], writes=[(name + "xr", si)], tag=f"{name}_ldxr{cg}_{tt}")
            ctx.op("vector", lambda e, pp=pp, ss=ss, xrs=xrs: e.tensor_add(
                out=ss, in0=pp[:, :CGW], in1=xrs),
                reads=[psk, (name + "xr", si)], writes=[(name + "stg", si)],
                tag=f"{name}_ev{cg}_{tt}")
            stslot = (cg * nt + tt) % 8
            ctx.dma("sync", lambda e, ss=ss, tt=tt, cg=cg: e.dma_start(
                out=x4_dram[tt * 128:(tt + 1) * 128, cg * CGW:(cg + 1) * CGW],
                in_=ss),
                reads=[(name + "stg", si)], writes=[(name + "x4st", stslot)],
                tag=f"{name}_stx4{cg}_{tt}")
    return [(name + "x4st", i) for i in range(8)]



def emit_l0_attn_a(ctx, xh_dram, inpw_dram, lnw_dram, mup_dram, convw_dram,
                   sz_dram, xs_dram, bT_dram, cT_dram, dtr_dram, ident, identF,
                   T=1024, D=4096, NH=64, name="a0"):
    nc = ctx.nc
    nt, nk = T // 128, D // 128
    Th = T + 3
    PROJ = 2 * D + 256 + NH
    nct = cdiv(PROJ, 128)
    CONV_CT = (D + 256) // 128

    FxT = nk * Th
    FxBC = CONV_CT * Th
    big = (T == 1024 and D == 4096)
    WSLW = 384 if big else 512
    SLOT = nk * WSLW
    if big:
        O = 44544
        xlnT = nc.alloc_sbuf_tensor_at(f"{name}_xlnT", [128, FxT], BF16, offset=O)
        xbcT = nc.alloc_sbuf_tensor_at(f"{name}_xbcT", [128, FxBC], BF16,
                                       offset=O + 65728)
        wsl = nc.alloc_sbuf_tensor_at(f"{name}_wsl", [128, 2 * SLOT], BF16,
                                      offset=O + 135616)
        xt_t = nc.alloc_sbuf_tensor_at(f"{name}_xt", [128, 2 * D], F32,
                                       offset=O + 65728)
        xln_t = nc.alloc_sbuf_tensor_at(f"{name}_xln", [128, 2 * D], BF16,
                                        offset=O + 98496)
    else:
        xlnT = nc.alloc_sbuf_tensor(f"{name}_xlnT", [128, FxT], BF16)
        xbcT = nc.alloc_sbuf_tensor(f"{name}_xbcT", [128, FxBC], BF16)
        wsl = nc.alloc_sbuf_tensor(f"{name}_wsl", [128, 2 * SLOT], BF16)
        xt_t = xln_t = None
    psA = nc.alloc_psum_tensor(f"{name}_psA", [128, 1536], F32)
    psB = nc.alloc_psum_tensor(f"{name}_psB", [128, 1536], F32)
    psZ = nc.alloc_psum_tensor(f"{name}_psZ", [128, 512], F32)
    KA, KB, KZ = (name + "psA",), (name + "psB",), (name + "psZ",)
    lnw = nc.alloc_sbuf_tensor(f"{name}_lnw", [128, nk], F32)
    mup = nc.alloc_sbuf_tensor(f"{name}_mup", [128, nct], F32)
    convw = nc.alloc_sbuf_tensor(f"{name}_cw", [128, 4 * CONV_CT], F32)
    zstg = nc.alloc_sbuf_tensor(f"{name}_zst", [128, T], F32)
    cacc = zstg
    zstg2 = nc.alloc_sbuf_tensor(f"{name}_zs2", [128, nt * 128], F32)
    dtst = nc.alloc_sbuf_tensor(f"{name}_dts", [NH, T], F32)

    ctx.dma("gpsimd", lambda e: e.dma_start(
        out=lnw[:, :], in_=lnw_dram.ap().rearrange("(a p) -> p a", p=128),
        allow_slow_non_contiguous=True),
        reads=[], writes=[(name + "lnw", 0)], tag=f"{name}_ldlnw")
    ctx.dma("gpsimd", lambda e: e.dma_start(
        out=mup[:, :nct - 1], in_=mup_dram[:(nct - 1) * 128].rearrange(
            "(a p) -> p a", p=128), allow_slow_non_contiguous=True),
        reads=[], writes=[(name + "mup", 0)], tag=f"{name}_ldmup")
    ctx.dma("gpsimd", lambda e: e.dma_start(
        out=mup[:NH, nct - 1:nct], in_=mup_dram[(nct - 1) * 128:].rearrange(
            "(a p) -> p a", p=NH), allow_slow_non_contiguous=True),
        reads=[], writes=[(name + "mup", 1)], tag=f"{name}_ldmup2")
    ctx.dma("gpsimd", lambda e: e.dma_start(
        out=convw[:, :].rearrange("p (a k) -> p a k", k=4),
        in_=convw_dram[:, :].rearrange(
            "(a p) k -> p a k", p=128), allow_slow_non_contiguous=True),
        reads=[], writes=[(name + "cw", 0)], tag=f"{name}_ldcw")

    def xsrc(tt):
        return xh_dram[3 + tt * 128: 3 + (tt + 1) * 128, :]

    norm = Norm(ctx, name + "n", xsrc, T + 128, D, rms2_eps=1e-5, nslots=2,
                xt_t=xt_t, xln_t=xln_t)
    for tt in range(nt):
        norm.emit_tile(tt)
        xlk = (norm.name + "xln", tt % norm.nslots)
        for kg in range(cdiv(nk, 8)):
            ps, psk = (psA, KA) if kg % 2 == 0 else (psB, KB)
            kts = list(range(kg * 8, min(nk, kg * 8 + 8)))
            for i, kt in enumerate(kts):
                ctx.op("tensor", lambda e, i=i, kt=kt, ps=ps, tt=tt: e.matmul(
                    ps[:, i * 128:(i + 1) * 128],
                    norm.xln_slot(tt)[:, kt * 128:(kt + 1) * 128], ident[:, :],
                    start=True, stop=True),
                    reads=[xlk, "ident"], writes=[psk], tag=f"{name}tp{tt}_{kt}")
            n = len(kts)
            dst = sap(xlnT, FxT, kts[0] * Th + 3 + tt * 128, [[Th, n], [1, 128]])
            wk = [("xlnT" + name, kt, tt) for kt in kts]
            ctx.op("vector", lambda e, ps=ps, dst=dst, n=n, kt0=kts[0]: e.tensor_mul(
                out=dst, in0=ps[:, :n * 128],
                in1=sap(lnw, nk, kt0, [[1, n], [0, 128]])),
                reads=[psk, (name + "lnw", 0)], writes=wk, tag=f"{name}te{tt}_{kg}")
    norm.x_src = lambda tt: xh_dram[0:3, :]
    norm.emit_tile(nt, rows=3)
    hk = (norm.name + "xln", nt % norm.nslots)
    hnorm = norm
    halo_keys = []
    for kg in range(cdiv(nk, 8)):
        ps, psk = (psA, KA) if kg % 2 == 0 else (psB, KB)
        kts = list(range(kg * 8, min(nk, kg * 8 + 8)))
        for i, kt in enumerate(kts):
            ctx.op("tensor", lambda e, i=i, kt=kt, ps=ps: e.matmul(
                ps[:, i * 8:(i * 8) + 3],
                hnorm.xln_slot(0)[:3, kt * 128:(kt + 1) * 128],
                ident[:3, :3], start=True, stop=True),
                reads=[hk, "ident"], writes=[psk], tag=f"{name}htp{kt}")
        n = len(kts)
        dst = sap(xlnT, FxT, kts[0] * Th, [[Th, n], [1, 3]])
        src_ap = sap(psA if kg % 2 == 0 else psB, 1536, 0, [[8, n], [1, 3]])
        ctx.op("vector", lambda e, dst=dst, src_ap=src_ap, n=n, kt0=kts[0]: e.tensor_mul(
            out=dst, in0=src_ap,
            in1=sap(lnw, nk, kt0, [[1, n], [0, 3]])),
            reads=[psk, (name + "lnw", 0)],
            writes=[("xlnTh" + name, kts[0])], tag=f"{name}hte{kg}")
        halo_keys.append(("xlnTh" + name, kts[0]))

    CHUNKS = []
    _o = 0
    while _o < Th:
        CHUNKS.append((_o, min(512, Th - _o)))
        _o += 512

    def rhs_keys(kt):
        return [("xlnT" + name, kt, tt) for tt in range(nt)] + list(halo_keys)

    for cg in range(cdiv(PROJ, WSLW)):
        c0 = cg * WSLW
        gw = min(WSLW, PROJ - c0)
        slot = cg % 2
        sl = wsl[:, slot * SLOT: slot * SLOT + nk * gw]
        src = inpw_dram[:, c0:c0 + gw].rearrange("(kt p) c -> p kt c", p=128)
        ctx.dma("gpsimd", lambda e, sl=sl, src=src, gw=gw: e.dma_start(
            out=sl.rearrange("p (kt c) -> p kt c", c=gw), in_=src),
            reads=[], writes=[(name + "wsl", slot)], tag=f"{name}_ldw{cg}")
        for ii in range(cdiv(gw, 128)):
            ct = (c0 + ii * 128) // 128
            cw_ = min(128, gw - ii * 128)
            pp, psk = (psA, KA) if ct % 2 == 0 else (psB, KB)
            for kt in range(nk):
                for (h0, hw_) in CHUNKS:
                    ctx.op("tensor", lambda e, pp=pp, kt=kt, h0=h0, hw_=hw_, sl=sl, ii=ii, gw=gw, cw_=cw_: e.matmul(
                        pp[:cw_, h0:h0 + hw_],
                        sl[:, kt * gw + ii * 128: kt * gw + ii * 128 + cw_],
                        xlnT[:, kt * Th + h0: kt * Th + h0 + hw_],
                        start=(kt == 0), stop=(kt == nk - 1)),
                        reads=[(name + "wsl", slot)] + rhs_keys(kt),
                        writes=[psk], tag=f"{name}_pmm{ct}_{kt}_{h0}")
            if ct < nk:
                zs = zstg[:, 0:T]
                ctx.op("scalar", lambda e, pp=pp, zs=zs, ct=ct: e.activation(
                    out=zs, in_=pp[:, 3:Th], func=AF.Silu,
                    scale=mup[:, ct:ct + 1]),
                    reads=[psk, (name + "mup", 0)], writes=[(name + "zst", 0)],
                    tag=f"{name}_zev{ct}")
                ngr = cdiv(nt, 4)
                for g in range(ngr):
                    nn_ = min(4, nt - g * 4)
                    for i in range(nn_):
                        tt2 = g * 4 + i
                        ctx.op("tensor", lambda e, zs=zs, i=i, tt2=tt2: e.matmul(
                            psZ[:, i * 128:(i + 1) * 128],
                            zs[:, tt2 * 128:(tt2 + 1) * 128], identF[:, :],
                            start=True, stop=True),
                            reads=[(name + "zst", 0), "identF"], writes=[KZ],
                            tag=f"{name}_ztp{ct}_{tt2}")
                    st = zstg2[:, g * 512:g * 512 + nn_ * 128]
                    ctx.op("vector", lambda e, st=st, nn_=nn_: e.tensor_copy(
                        out=st, in_=psZ[:, :nn_ * 128]),
                        reads=[KZ], writes=[(name + "zst2", g)],
                        tag=f"{name}_zte{ct}_{g}")
                dst = sz_dram[:, ct * 128:(ct + 1) * 128].rearrange(
                    "(tt p) c -> p tt c", p=128)
                src8 = zstg2[:, :nt * 128].rearrange("p (tt c) -> p tt c", c=128)
                ctx.dma("sync", lambda e, dst=dst, src8=src8: e.dma_start(
                    out=dst, in_=src8),
                    reads=[(name + "zst2", g2) for g2 in range(cdiv(nt, 4))],
                    writes=[(name + "szs", ct % 2)], tag=f"{name}_zstr{ct}")
            elif ct < nk + CONV_CT:
                cct = ct - nk
                dstx = xbcT[:, cct * Th: (cct + 1) * Th]
                ctx.op("vector", lambda e, pp=pp, dstx=dstx, ct=ct: e.tensor_scalar(
                    out=dstx, in0=pp[:, :Th], scalar1=mup[:, ct:ct + 1],
                    scalar2=None, op0=ALU.mult, op1=ALU.bypass),
                    reads=[psk, (name + "mup", 0)], writes=[(name + "xbc", cct)],
                    tag=f"{name}_bev{ct}")
            else:
                ctx.op("vector", lambda e, pp=pp: e.tensor_scalar(
                    out=dtst[:, :], in0=pp[:NH, 3:Th], scalar1=mup[:NH, nct - 1:nct],
                    scalar2=None, op0=ALU.mult, op1=ALU.bypass),
                    reads=[psk, (name + "mup", 1)], writes=[(name + "dts", 0)],
                    tag=f"{name}_dev")
                ctx.dma("sync", lambda e: e.dma_start(
                    out=dtr_dram[:, :], in_=dtst[:, :]),
                    reads=[(name + "dts", 0)], writes=[(name + "dstr", 0)],
                    tag=f"{name}_dstr")

    sil = nc.alloc_sbuf_tensor(f"{name}_sil", [128, 2 * T], BF16)
    xstg = nc.alloc_sbuf_tensor(f"{name}_xst", [128, nt * 128], BF16)
    for cct in range(CONV_CT):
        src = xbcT[:, cct * Th: (cct + 1) * Th]
        ca = cacc[:, 0:T]
        eng = "vector"
        ctx.op(eng, lambda e, src=src, ca=ca, cct=cct: e.tensor_scalar(
            out=ca, in0=src[:, 0:T], scalar1=convw[:, cct * 4:cct * 4 + 1],
            scalar2=None, op0=ALU.mult, op1=ALU.bypass),
            reads=[(name + "xbc", cct), (name + "cw", 0),
                   (name + "zst", 0), (name + "szs", 0), (name + "szs", 1)],
            writes=[(name + "cacc", 0)], tag=f"{name}_cv0_{cct}")
        for k in range(1, 4):
            ctx.op(eng, lambda e, src=src, ca=ca, cct=cct, k=k: e.scalar_tensor_tensor(
                out=ca, in0=src[:, k:k + T], scalar=convw[:, cct * 4 + k:cct * 4 + k + 1],
                in1=ca, op0=ALU.mult, op1=ALU.add),
                reads=[(name + "xbc", cct), (name + "cw", 0), (name + "cacc", 0)],
                writes=[(name + "cacc", 0)], tag=f"{name}_cv{k}_{cct}")
        sl_ = sil[:, (cct % 2) * T:(cct % 2) * T + T]
        ctx.op("scalar", lambda e, ca=ca, sl_=sl_: e.activation(
            out=sl_, in_=ca, func=AF.Silu),
            reads=[(name + "cacc", 0)], writes=[(name + "sil", cct % 2)],
            tag=f"{name}_cs{cct}")
        if cct < D // 128:
            ngr = cdiv(nt, 4)
            for g in range(ngr):
                pp, psk = (psA, KA) if g % 2 == 0 else (psB, KB)
                for i in range(min(4, nt - g * 4)):
                    tt = g * 4 + i
                    ctx.op("tensor", lambda e, sl_=sl_, pp=pp, i=i, tt=tt: e.matmul(
                        pp[:, i * 128:(i + 1) * 128],
                        sl_[:, tt * 128:(tt + 1) * 128], ident[:, :],
                        start=True, stop=True),
                        reads=[(name + "sil", cct % 2), "ident"], writes=[psk],
                        tag=f"{name}_xtp{cct}_{tt}")
                nn_ = min(4, nt - g * 4) * 128
                st = xstg[:, g * 512:g * 512 + nn_]
                ctx.op("vector", lambda e, pp=pp, st=st, nn_=nn_: e.tensor_copy(
                    out=st, in_=pp[:, :nn_]),
                    reads=[psk], writes=[(name + "xstg", g)], tag=f"{name}_xte{cct}_{g}")
            dst = xs_dram[:, cct * 128:(cct + 1) * 128].rearrange(
                "(tt p) c -> p tt c", p=128)
            src8 = xstg[:, :nt * 128].rearrange("p (tt c) -> p tt c", c=128)
            ctx.dma("sync", lambda e, dst=dst, src8=src8: e.dma_start(
                out=dst, in_=src8),
                reads=[(name + "xstg", g2) for g2 in range(cdiv(nt, 4))],
                writes=[(name + "xsst", cct % 2)], tag=f"{name}_xst{cct}")
        else:
            dd = bT_dram if cct == D // 128 else cT_dram
            ctx.dma("sync", lambda e, sl_=sl_, dd=dd: e.dma_start(
                out=dd[:, :], in_=sl_),
                reads=[(name + "sil", cct % 2)], writes=[(name + "bcst", cct)],
                tag=f"{name}_bcs{cct}")
    fin = ([(name + "szs", i) for i in range(2)] + [(name + "dstr", 0)] +
           [(name + "xsst", i) for i in range(2)] +
           [(name + "bcst", D // 128), (name + "bcst", D // 128 + 1)])
    return fin



def emit_l0_scan(ctx, xs_dram, bT_dram, cT_dram, dtr_dram, dtb_dram,
                 alog_dram, drow_dram, y_dram, ident, identF, causalT_dram,
                 L=8192, NHH=8, HD=64, name="s0"):
    nc = ctx.nc
    NC = L // 128
    W = NHH * HD
    bT = nc.alloc_sbuf_tensor(f"{name}_bT", [128, L], BF16)
    cT = nc.alloc_sbuf_tensor(f"{name}_cT", [128, L], BF16)
    assert NHH <= 32
    DTt = nc.alloc_sbuf_tensor(f"{name}_DT", [NHH, L], F32)
    Gt = nc.alloc_sbuf_tensor(f"{name}_G", [NHH, L], F32)
    s0t = nc.alloc_sbuf_tensor(f"{name}_S0", [NHH, L], F32)
    s1t = nc.alloc_sbuf_tensor(f"{name}_S1", [NHH, L], F32)
    lec = nc.alloc_sbuf_tensor(f"{name}_lec", [NHH, 512], F32)
    dtb = nc.alloc_sbuf_tensor(f"{name}_dtb", [NHH, 1], F32)
    alog = nc.alloc_sbuf_tensor(f"{name}_al", [NHH, 2], F32)
    drow = nc.alloc_sbuf_tensor(f"{name}_dr", [1, NHH], F32)
    zs128 = nc.alloc_sbuf_tensor(f"{name}_z", [NHH, 128], F32)
    onesb = nc.alloc_sbuf_tensor(f"{name}_ob", [1, 128], BF16)
    onesF = nc.alloc_sbuf_tensor(f"{name}_of", [1, 128], F32)
    causT = nc.alloc_sbuf_tensor(f"{name}_ca", [128, 128], BF16)
    grow = nc.alloc_sbuf_tensor(f"{name}_gr", [1, 2 * NHH * 128], F32)
    grow2 = nc.alloc_sbuf_tensor(f"{name}_gr2", [1, NHH * (L // 128)], F32)
    gam = nc.alloc_sbuf_tensor(f"{name}_gam", [128, NHH * NC], F32)
    lamT = nc.alloc_sbuf_tensor(f"{name}_lT", [128, 2 * NHH], F32)
    eT = nc.alloc_sbuf_tensor(f"{name}_eT", [128, 2 * NHH], F32)
    wT = nc.alloc_sbuf_tensor(f"{name}_wT", [128, 2 * NHH], F32)
    db = nc.alloc_sbuf_tensor(f"{name}_db", [128, NHH], F32)
    mm_ = nc.alloc_sbuf_tensor(f"{name}_mm", [128, 2 * 128], BF16)
    btok = nc.alloc_sbuf_tensor(f"{name}_btk", [128, 2 * 128], BF16)
    warg = nc.alloc_sbuf_tensor(f"{name}_wa", [128, 2 * 128], F32)
    wdec = nc.alloc_sbuf_tensor(f"{name}_wd", [128, 2 * 128], BF16)
    wfin = nc.alloc_sbuf_tensor(f"{name}_wf", [128, 2 * NHH * 128], BF16)
    btl = nc.alloc_sbuf_tensor(f"{name}_btl", [128, 2 * NHH * 128], BF16)
    xsb = nc.alloc_sbuf_tensor(f"{name}_xs", [128, 2 * W], BF16)
    H = nc.alloc_sbuf_tensor(f"{name}_H", [128, W], F32)
    Hb = nc.alloc_sbuf_tensor(f"{name}_Hb", [128, W], BF16)
    yst = nc.alloc_sbuf_tensor(f"{name}_yst", [128, 2 * W], F32)
    t1 = nc.alloc_sbuf_tensor(f"{name}_t1", [128, 2 * W], F32)
    wfb = nc.alloc_sbuf_tensor(f"{name}_wfb", [NHH, 128], F32)
    psA = nc.alloc_psum_tensor(f"{name}_psA", [128, 512], F32)
    psB = nc.alloc_psum_tensor(f"{name}_psB", [128, 512], F32)
    psC = nc.alloc_psum_tensor(f"{name}_psC", [128, 512], F32)
    psD = nc.alloc_psum_tensor(f"{name}_psD", [128, 512], F32)
    KA, KB, KC, KD = (name + "A",), (name + "B",), (name + "C",), (name + "D",)

    for d_, s_, kk, q_ in ((bT_dram, bT, "bT", "sync"), (cT_dram, cT, "cT", "sync"),
                           (causalT_dram, causT, "ca", "gpsimd")):
        ctx.dma(q_, lambda e, d_=d_, s_=s_: e.dma_start(out=s_[:, :], in_=d_[:, :]),
                reads=[], writes=[(name + kk, 0)], tag=f"{name}_ld{kk}")
    ctx.dma("gpsimd", lambda e: e.dma_start(out=DT, in_=dtr_dram[:, :]),
            reads=[], writes=[(name + "dtr", 0)], tag=f"{name}_lddtr")
    for d_, s_, kk in ((dtb_dram, dtb, "dtb"), (alog_dram, alog, "al"),
                       (drow_dram, drow, "dr")):
        dst_ap = s_[:, :] if kk == "dr" else s_[:, :1]
        ctx.dma("gpsimd", lambda e, d_=d_, dst_ap=dst_ap: e.dma_start(
            out=dst_ap, in_=d_[:, :]),
            reads=[], writes=[(name + kk, 0)], tag=f"{name}_ld{kk}")
    ctx.op("gpsimd", lambda e: e.memset(onesb[:, :], 1.0), reads=[],
           writes=[(name + "ones", 0)], tag=f"{name}_on")
    ctx.op("gpsimd", lambda e: e.memset(onesF[:, :], 1.0), reads=[],
           writes=[(name + "onesF", 0)], tag=f"{name}_onf")
    ctx.op("gpsimd", lambda e: e.memset(Hb[:, :], 0.0), reads=[],
           writes=[(name + "Hb", 0)], tag=f"{name}_hb0")
    ctx.op("gpsimd", lambda e: e.memset(zs128[:, :], 0.0), reads=[],
           writes=[(name + "z0", 0)], tag=f"{name}_z0")
    ctx.op("gpsimd", lambda e: e.memset(H[:, :], 0.0), reads=[],
           writes=[(name + "H", 0)], tag=f"{name}_h0")

    one_ap, one_key = ctx.constf(1.0)
    DT = DTt[:, 0:L]
    G = Gt[:, 0:L]

    def Gsl(c):
        return Gt[:, c * 128:(c + 1) * 128]

    def DTsl(c):
        return DTt[:, c * 128:(c + 1) * 128]

    s0 = s0t[:, 0:L]
    s1 = s1t[:, 0:L]
    ctx.op("scalar", lambda e: e.activation(out=s0, in_=DT, func=AF.Abs,
                                            bias=dtb[:, :1]),
           reads=[(name + "dtr", 0), (name + "dtb", 0)], writes=[(name + "s0", 0)],
           tag=f"{name}_ab")
    ctx.op("scalar", lambda e: e.activation(out=s0, in_=s0, func=AF.Exp, scale=-1.0),
           reads=[(name + "s0", 0)], writes=[(name + "s0", 0)], tag=f"{name}_ex")
    ctx.op("scalar", lambda e: e.activation(out=s0, in_=s0, func=AF.Ln,
                                            bias=one_ap[:NHH, :1]),
           reads=[(name + "s0", 0), one_key], writes=[(name + "s0", 0)],
           tag=f"{name}_ln")
    ctx.op("scalar", lambda e: e.activation(out=s1, in_=DT, func=AF.Relu,
                                            bias=dtb[:, :1]),
           reads=[(name + "dtr", 0), (name + "dtb", 0)], writes=[(name + "s1", 0)],
           tag=f"{name}_rl")
    ctx.op("vector", lambda e: e.tensor_add(out=DT, in0=s0, in1=s1),
           reads=[(name + "s0", 0), (name + "s1", 0)], writes=[(name + "dt", 0)],
           tag=f"{name}_sp")
    ctx.op("scalar", lambda e: e.activation(out=alog[:, 1:2], in_=alog[:, :1],
                                            func=AF.Exp),
           reads=[(name + "al", 0)], writes=[(name + "A0", 0)], tag=f"{name}_ea")
    ctx.op("vector", lambda e: e.tensor_scalar(
        out=alog[:, 1:2], in0=alog[:, 1:2], scalar1=-1.0, scalar2=None,
        op0=ALU.mult, op1=ALU.bypass),
        reads=[(name + "A0", 0)], writes=[(name + "A", 0)], tag=f"{name}_na")
    ctx.op("vector", lambda e: e.tensor_scalar(
        out=s0, in0=DT, scalar1=alog[:, 1:2], scalar2=None, op0=ALU.mult,
        op1=ALU.bypass),
        reads=[(name + "dt", 0), (name + "A", 0)], writes=[(name + "g", 0)],
        tag=f"{name}_g")
    for c in range(NC):
        ctx.op("vector", lambda e, c=c: e.tensor_tensor_scan(
            out=Gsl(c), data0=s0t[:, c * 128:(c + 1) * 128],
            data1=zs128[:, :], initial=0.0, op0=ALU.add, op1=ALU.add),
            reads=[(name + "g", 0), (name + "z0", 0)], writes=[(name + "G", c)],
            tag=f"{name}_cs{c}")
    gkeys = [(name + "G", c) for c in range(NC)]
    ctx.op("vector", lambda e: e.tensor_copy(
        out=wfb[:, :NC], in_=bass.AP(Gt, 127, [[L, NHH], [128, NC]])),
        reads=gkeys, writes=[(name + "ge", 0)], tag=f"{name}_ge")
    ctx.op("scalar", lambda e: e.activation(out=wfb[:, NC:2 * NC], in_=wfb[:, :NC],
                                            func=AF.Exp),
           reads=[(name + "ge", 0)], writes=[(name + "gam0", 0)], tag=f"{name}_gx")
    ctx.dma("gpsimd", lambda e: e.dma_start(
        out=grow2[:1, :NHH * NC], in_=wfb[:, NC:2 * NC]),
        reads=[(name + "gam0", 0)], writes=[(name + "grow2", 0)],
        tag=f"{name}_gf")
    ctx.op("tensor", lambda e: e.matmul(
        psD[:, :NHH * NC], onesF[:1, :], grow2[:1, :NHH * NC],
        start=True, stop=True),
        reads=[(name + "grow2", 0), (name + "onesF", 0)], writes=[KD],
        tag=f"{name}_gb")
    ctx.op("vector", lambda e: e.tensor_copy(out=gam[:, :], in_=psD[:, :NHH * NC]),
           reads=[KD], writes=[(name + "gam", 0)], tag=f"{name}_gc")
    ctx.op("tensor", lambda e: e.matmul(psD[:, :NHH], onesF[:1, :], drow[:1, :],
                                        start=True, stop=True),
           reads=[(name + "dr", 0), (name + "onesF", 0)], writes=[KD],
           tag=f"{name}_dbm")
    ctx.op("vector", lambda e: e.tensor_copy(out=db[:, :], in_=psD[:, :NHH]),
           reads=[KD], writes=[(name + "db", 0)], tag=f"{name}_dbc")

    nb4 = NHH // 4
    for c in range(NC):
        sl = c % 2
        xc = xsb[:, sl * W:(sl + 1) * W]
        ctx.dma("sync", lambda e, xc=xc, c=c: e.dma_start(
            out=xc, in_=xs_dram[c * 128:(c + 1) * 128, :]),
            reads=[], writes=[(name + "xs", sl)], tag=f"{name}_lx{c}")
        gr = grow[:1, sl * NHH * 128:(sl + 1) * NHH * 128]
        ctx.dma("gpsimd", lambda e, gr=gr, c=c: e.dma_start(
            out=gr, in_=Gsl(c)),
            reads=gkeys, writes=[(name + "gr", sl)], tag=f"{name}_gr{c}")
        ctx.op("tensor", lambda e, c=c: e.matmul(
            psC[:, :128], bT[:, c * 128:(c + 1) * 128],
            cT[:, c * 128:(c + 1) * 128], start=True, stop=True),
            reads=[(name + "bT", 0), (name + "cT", 0)], writes=[KC],
            tag=f"{name}_M{c}")
        mmc = mm_[:, sl * 128:(sl + 1) * 128]
        ctx.op("vector", lambda e, mmc=mmc: e.tensor_mul(
            out=mmc, in0=psC[:, :128], in1=causT[:, :]),
            reads=[KC, (name + "ca", 0)], writes=[(name + "mm", sl)],
            tag=f"{name}_Mm{c}")
        ctx.op("tensor", lambda e, c=c: e.matmul(
            psC[:, 128:256], bT[:, c * 128:(c + 1) * 128], ident[:, :],
            start=True, stop=True),
            reads=[(name + "bT", 0), "ident"], writes=[KC], tag=f"{name}_Bt{c}")
        btc = btok[:, sl * 128:(sl + 1) * 128]
        ctx.op("vector", lambda e, btc=btc: e.tensor_copy(
            out=btc, in_=psC[:, 128:256]),
            reads=[KC], writes=[(name + "btk", sl)], tag=f"{name}_Bte{c}")
        ctx.op("scalar", lambda e, c=c: e.activation(
            out=lec[:, 0:128], in_=DTsl(c), func=AF.Ln),
            reads=[(name + "dt", 0)], writes=[(name + "lec", 0)],
            tag=f"{name}_lc{c}")
        ctx.op("vector", lambda e, c=c: e.tensor_sub(
            out=lec[:, 0:128], in0=lec[:, 0:128], in1=Gsl(c)),
            reads=[(name + "lec", 0)] + gkeys, writes=[(name + "lec", 0)],
            tag=f"{name}_lc2{c}")
        ctx.op("scalar", lambda e, c=c: e.activation(
            out=lec[:, 128:256], in_=Gsl(c), func=AF.Exp),
            reads=gkeys, writes=[(name + "lec", 1)], tag=f"{name}_ec{c}")
        for (off_, dstb, kk) in ((0, lamT, "lT"), (128, eT, "eT")):
            ctx.op("tensor", lambda e, off_=off_: e.matmul(
                psD[:, :NHH], lec[:, off_:off_ + 128], identF[:NHH, :NHH],
                start=True, stop=True),
                reads=[(name + "lec", 0), (name + "lec", 1), "identF"],
                writes=[KD], tag=f"{name}_{kk}t{c}")
            ctx.op("vector", lambda e, dstb=dstb, sl=sl: e.tensor_copy(
                out=dstb[:, sl * NHH:(sl + 1) * NHH], in_=psD[:, :NHH]),
                reads=[KD], writes=[(name + kk, sl)], tag=f"{name}_{kk}e{c}")
        ctx.op("scalar", lambda e, c=c: e.activation(
            out=wfb[:, :128], in_=lec[:, 0:128], func=AF.Exp,
            bias=bass.AP(Gt, c * 128 + 127, [[L, NHH], [1, 1]])),
            reads=[(name + "lec", 0), (name + "gam", 0)] + gkeys,
            writes=[(name + "wf", 0)], tag=f"{name}_wf{c}")
        ctx.op("tensor", lambda e: e.matmul(
            psD[:, :NHH], wfb[:, :128], identF[:NHH, :NHH], start=True, stop=True),
            reads=[(name + "wf", 0), "identF"], writes=[KD], tag=f"{name}_wft{c}")
        ctx.op("vector", lambda e, sl=sl: e.tensor_copy(
            out=wT[:, sl * NHH:(sl + 1) * NHH], in_=psD[:, :NHH]),
            reads=[KD], writes=[(name + "wT", sl)], tag=f"{name}_wfe{c}")

        for r in range(NHH):
            ctx.op("tensor", lambda e, gr=gr, r=r: e.matmul(
                psD[:, :128], onesF[:1, :], gr[:1, r * 128:(r + 1) * 128],
                start=True, stop=True),
                reads=[(name + "gr", sl), (name + "onesF", 0)], writes=[KD],
                tag=f"{name}_gb{c}_{r}")
            wa = warg[:, sl * 128:(sl + 1) * 128]
            ctx.op("vector", lambda e, wa=wa, r=r, sl=sl: e.tensor_scalar(
                out=wa, in0=psD[:, :128],
                scalar1=lamT[:, sl * NHH + r: sl * NHH + r + 1], scalar2=15.0,
                op0=ALU.add, op1=ALU.min),
                reads=[KD, (name + "lT", sl)], writes=[(name + "wa", sl)],
                tag=f"{name}_wa{c}_{r}")
            wd = wdec[:, sl * 128:(sl + 1) * 128]
            ctx.op("scalar", lambda e, wa=wa, wd=wd: e.activation(
                out=wd, in_=wa, func=AF.Exp),
                reads=[(name + "wa", sl)], writes=[(name + "wd", sl)],
                tag=f"{name}_wx{c}_{r}")
            wf_ = wfin[:, (sl * NHH + r) * 128:(sl * NHH + r + 1) * 128]
            ctx.op("vector", lambda e, wf_=wf_, wd=wd, mmc=mmc: e.tensor_mul(
                out=wf_, in0=mmc, in1=wd),
                reads=[(name + "wd", sl), (name + "mm", sl)],
                writes=[(name + "wfin", sl, r)], tag=f"{name}_wm{c}_{r}")
            ctx.op("vector", lambda e, wf_=wf_, r=r, sl=sl: e.scalar_tensor_tensor(
                out=wf_, in0=ident[:, :], scalar=db[:, r:r + 1], in1=wf_,
                op0=ALU.mult, op1=ALU.add),
                reads=[(name + "wfin", sl, r), (name + "db", 0), "ident"],
                writes=[(name + "wfin", sl, r)], tag=f"{name}_wdg{c}_{r}")
            bt_ = btl[:, (sl * NHH + r) * 128:(sl * NHH + r + 1) * 128]
            ctx.op("vector", lambda e, bt_=bt_, btc=btc, r=r, sl=sl: e.tensor_scalar(
                out=bt_, in0=btc, scalar1=wT[:, sl * NHH + r: sl * NHH + r + 1],
                scalar2=None, op0=ALU.mult, op1=ALU.bypass),
                reads=[(name + "btk", sl), (name + "wT", sl)],
                writes=[(name + "btl", sl, r)], tag=f"{name}_btl{c}_{r}")
        for b in range(nb4):
            r0 = b * 4
            pY1, kY1 = (psA, KA) if b % 2 == 0 else (psB, KB)
            for j in range(4):
                r = r0 + j
                ctx.op("tensor", lambda e, r=r, j=j, sl=sl, pY1=pY1, xc=xc: e.matmul(
                    pY1[:, j * HD:(j + 1) * HD],
                    wfin[:, (sl * NHH + r) * 128:(sl * NHH + r + 1) * 128],
                    xc[:, r * HD:(r + 1) * HD], start=(j == 0), stop=(j == 3)),
                    reads=[(name + "wfin", sl, r), (name + "xs", sl)],
                    writes=[kY1], tag=f"{name}_y1{c}_{r}")
            for j in range(4):
                r = r0 + j
                ctx.op("tensor", lambda e, r=r, j=j, c=c: e.matmul(
                    psC[:, 256 + j * HD:256 + (j + 1) * HD],
                    cT[:, c * 128:(c + 1) * 128], Hb[:, r * HD:(r + 1) * HD],
                    start=(j == 0), stop=(j == 3)),
                    reads=[(name + "cT", 0), (name + "Hb", 0)], writes=[KC],
                    tag=f"{name}_y2{c}_{r}")
            ys = yst[:, sl * W + r0 * HD: sl * W + (r0 + 4) * HD]
            tt1 = t1[:, sl * W + r0 * HD: sl * W + (r0 + 4) * HD]
            ebc = sap(eT, 2 * NHH, sl * NHH + r0, [[1, 4], [0, HD]])
            ctx.op("vector", lambda e, tt1=tt1, ebc=ebc: e.tensor_mul(
                out=tt1, in0=psC[:, 256:256 + 4 * HD], in1=ebc),
                reads=[KC, (name + "eT", sl)], writes=[(name + "t1", sl, b)],
                tag=f"{name}_ye{c}_{b}")
            ctx.op("vector", lambda e, ys=ys, tt1=tt1, pY1=pY1: e.tensor_add(
                out=ys, in0=pY1[:, :4 * HD], in1=tt1),
                reads=[kY1, (name + "t1", sl, b)], writes=[(name + "yst", sl, b)],
                tag=f"{name}_ya{c}_{b}")
            for j in range(4):
                r = r0 + j
                ctx.op("tensor", lambda e, r=r, j=j, sl=sl, xc=xc: e.matmul(
                    psD[:, 256 + j * HD:256 + (j + 1) * HD],
                    btl[:, (sl * NHH + r) * 128:(sl * NHH + r + 1) * 128],
                    xc[:, r * HD:(r + 1) * HD], start=(j == 0), stop=(j == 3)),
                    reads=[(name + "btl", sl, r), (name + "xs", sl)],
                    writes=[KD], tag=f"{name}_sc{c}_{r}")
            hsl = H[:, r0 * HD:(r0 + 4) * HD]
            gbc = sap(gam, NHH * NC, r0 * NC + c, [[NC, 4], [0, HD]])
            ctx.op("vector", lambda e, hsl=hsl, gbc=gbc: e.tensor_mul(
                out=hsl, in0=hsl, in1=gbc),
                reads=[(name + "H", 0), (name + "gam", 0)], writes=[(name + "H", 0)],
                tag=f"{name}_hg{c}_{b}")
            ctx.op("vector", lambda e, hsl=hsl: e.tensor_add(
                out=hsl, in0=hsl, in1=psD[:, 256:256 + 4 * HD]),
                reads=[(name + "H", 0), KD], writes=[(name + "H", 0)],
                tag=f"{name}_hs{c}_{b}")
        ctx.op("scalar", lambda e: e.copy(out=Hb[:, :], in_=H[:, :]),
               reads=[(name + "H", 0)], writes=[(name + "Hb", 0)],
               tag=f"{name}_hb{c}")
        ysl = yst[:, sl * W:(sl + 1) * W]
        ctx.dma("scalar", lambda e, ysl=ysl, c=c: e.dma_start(
            out=y_dram[c * 128:(c + 1) * 128, :], in_=ysl),
            reads=[(name + "yst", sl, b2) for b2 in range(nb4)],
            writes=[(name + "yd", sl)], tag=f"{name}_sy{c}")
    return [(name + "yd", 0), (name + "yd", 1)]



def emit_gate_outproj(ctx, y_dram, g_dram, wT_dram, fold_dram, xres_dram,
                      xout_dram, ident, T=1024, D=4096, rms=False,
                      name="c0"):
    nc = ctx.nc
    nt, nk = T // 128, D // 128
    FxT = nk * T
    yT = nc.alloc_sbuf_tensor(f"{name}_yT", [128, FxT], BF16)
    SLOT = nk * 512
    wsl = nc.alloc_sbuf_tensor(f"{name}_wsl", [128, 2 * SLOT], BF16)
    psA = nc.alloc_psum_tensor(f"{name}_psA", [128, 1024], F32)
    psB = nc.alloc_psum_tensor(f"{name}_psB", [128, 1024], F32)
    KA, KB = (name + "psA",), (name + "psB",)
    yb = nc.alloc_sbuf_tensor(f"{name}_yb", [128, 2 * D], F32)
    gb = nc.alloc_sbuf_tensor(f"{name}_gb", [128, D], F32)
    e3 = nc.alloc_sbuf_tensor(f"{name}_e3", [128, 2 * D], BF16)
    fold = nc.alloc_sbuf_tensor(f"{name}_fo", [128, nk], F32)
    stt_ = nc.alloc_sbuf_tensor(f"{name}_st", [128, 4 * nt], F32)
    stg = nc.alloc_sbuf_tensor(f"{name}_stg", [128, 3 * 512], F32)
    xr = nc.alloc_sbuf_tensor(f"{name}_xr", [128, 3 * 512], F32)

    ctx.dma("gpsimd", lambda e: e.dma_start(out=fold[:, :], in_=fold_dram[:, :]),
            reads=[], writes=[(name + "fo", 0)], tag=f"{name}_ldf")
    eps_ap, eps_key = ctx.constf(1e-5)

    for tt in range(nt):
        sl = tt % 2
        ybs = yb[:, sl * D:(sl + 1) * D]
        gbs = gb[:, 0:D]
        e3s = e3[:, sl * D:(sl + 1) * D]
        ctx.dma("sync", lambda e, ybs=ybs, tt=tt: e.dma_start(
            out=ybs, in_=y_dram[tt * 128:(tt + 1) * 128, :]),
            reads=[], writes=[(name + "yb", sl)], tag=f"{name}_ly{tt}")
        ctx.dma("sync", lambda e, gbs=gbs, tt=tt: e.dma_start(
            out=gbs, in_=g_dram[tt * 128:(tt + 1) * 128, :]),
            reads=[], writes=[(name + "gb", 0)], tag=f"{name}_lg{tt}")
        ctx.op("vector", lambda e, ybs=ybs, gbs=gbs: e.tensor_mul(
            out=ybs, in0=ybs, in1=gbs),
            reads=[(name + "yb", sl), (name + "gb", 0)],
            writes=[(name + "yb", sl)], tag=f"{name}_e3{tt}")
        c = 4 * tt
        if rms:
            ctx.op("scalar", lambda e, ybs=ybs, e3s=e3s, c=c: e.activation(
                out=e3s, in_=ybs, func=AF.Square,
                accum_out=stt_[:, c:c + 1]),
                reads=[(name + "yb", sl)], writes=[(name + "e3", sl), (name, "ss", tt)],
                tag=f"{name}_sq{tt}")
            ctx.op("scalar", lambda e, c=c: e.activation(
                out=stt_[:, c + 1:c + 2], in_=stt_[:, c:c + 1], func=AF.Ln,
                bias=eps_ap[:, :1], scale=1.0 / D),
                reads=[(name, "ss", tt), eps_key], writes=[(name, "l", tt)],
                tag=f"{name}_l{tt}")
            ctx.op("scalar", lambda e, c=c: e.activation(
                out=stt_[:, c + 1:c + 2], in_=stt_[:, c + 1:c + 2], func=AF.Exp,
                scale=-0.5),
                reads=[(name, "l", tt)], writes=[(name, "rs", tt)],
                tag=f"{name}_r{tt}")
            ctx.op("scalar", lambda e, ybs=ybs, e3s=e3s, c=c: e.activation(
                out=e3s, in_=ybs, func=AF.Identity,
                scale=stt_[:, c + 1:c + 2]),
                reads=[(name + "yb", sl), (name, "rs", tt)],
                writes=[(name + "e3", sl)], tag=f"{name}_ap{tt}")
        else:
            ctx.op("scalar", lambda e, ybs=ybs, e3s=e3s: e.activation(
                out=e3s, in_=ybs, func=AF.Copy),
                reads=[(name + "yb", sl)], writes=[(name + "e3", sl)],
                tag=f"{name}_cp{tt}")
        for kg in range(cdiv(nk, 8)):
            ps, psk = (psA, KA) if kg % 2 == 0 else (psB, KB)
            kts = list(range(kg * 8, min(nk, kg * 8 + 8)))
            for i, kt in enumerate(kts):
                ctx.op("tensor", lambda e, i=i, kt=kt, ps=ps, e3s=e3s: e.matmul(
                    ps[:, i * 128:(i + 1) * 128],
                    e3s[:, kt * 128:(kt + 1) * 128], ident[:, :],
                    start=True, stop=True),
                    reads=[(name + "e3", sl), "ident"], writes=[psk],
                    tag=f"{name}tp{tt}_{kt}")
            n = len(kts)
            dst = sap(yT, FxT, kts[0] * T + tt * 128, [[T, n], [1, 128]])
            wk = [("yT" + name, kt, tt) for kt in kts]
            ctx.op("vector", lambda e, ps=ps, dst=dst, n=n, kt0=kts[0]: e.tensor_mul(
                out=dst, in0=ps[:, :n * 128],
                in1=sap(fold, nk, kt0, [[1, n], [0, 128]])),
                reads=[psk, (name + "fo", 0)], writes=wk, tag=f"{name}te{tt}_{kg}")

    for cg in range(D // 512):
        slot = cg % 2
        sl2 = wsl[:, slot * SLOT: slot * SLOT + nk * 512]
        src2 = wT_dram[:, cg * 512:(cg + 1) * 512].rearrange(
            "(kt p) c -> p kt c", p=128)
        ctx.dma("gpsimd", lambda e, sl2=sl2, src2=src2: e.dma_start(
            out=sl2.rearrange("p (kt c) -> p kt c", c=512), in_=src2),
            reads=[], writes=[(name + "wsl", slot)], tag=f"{name}_ldw{cg}")
        for tt in range(nt):
            pp, psk = (psA, KA) if tt % 2 == 0 else (psB, KB)
            for kt in range(nk):
                ctx.op("tensor", lambda e, pp=pp, kt=kt, tt=tt, sl2=sl2: e.matmul(
                    pp[:, :512], yT[:, kt * T + tt * 128: kt * T + tt * 128 + 128],
                    sl2[:, kt * 512:(kt + 1) * 512],
                    start=(kt == 0), stop=(kt == nk - 1)),
                    reads=[(name + "wsl", slot), ("yT" + name, kt, tt)],
                    writes=[psk], tag=f"{name}_mm{cg}_{tt}_{kt}")
            si = (cg * nt + tt) % 3
            ss = stg[:, si * 512:(si + 1) * 512]
            xrs = xr[:, si * 512:(si + 1) * 512]
            ctx.dma("scalar", lambda e, xrs=xrs, tt=tt, cg=cg: e.dma_start(
                out=xrs, in_=xres_dram[tt * 128:(tt + 1) * 128,
                                       cg * 512:(cg + 1) * 512]),
                reads=[], writes=[(name + "xr", si)], tag=f"{name}_lxr{cg}_{tt}")
            ctx.op("vector", lambda e, pp=pp, ss=ss, xrs=xrs: e.tensor_add(
                out=ss, in0=pp[:, :512], in1=xrs),
                reads=[psk, (name + "xr", si)], writes=[(name + "stg", si)],
                tag=f"{name}_ev{cg}_{tt}")
            stslot = (cg * nt + tt) % 8
            ctx.dma("sync", lambda e, ss=ss, tt=tt, cg=cg: e.dma_start(
                out=xout_dram[tt * 128:(tt + 1) * 128, cg * 512:(cg + 1) * 512],
                in_=ss),
                reads=[(name + "stg", si)], writes=[(name + "xo", stslot)],
                tag=f"{name}_so{cg}_{tt}")
    return [(name + "xo", i) for i in range(8)]



def emit_l1_attn_a(ctx, x2_dram, wqT_dram, wkT_dram, wvT_dram, wgT_dram,
                   winT_dram, cosT_dram, sinT_dram, rsign_dram,
                   qT_dram, kT_dram, v_dram, sg_dram, dtl_dram,
                   ident, identF, T=1024, D=4096, H=32, KVH=8, name="b0"):
    nc = ctx.nc
    nt, nk = T // 128, D // 128
    FxT = nk * T
    xlnT = nc.alloc_sbuf_tensor(f"{name}_xlnT", [128, FxT], BF16)
    WSLW = 256 if (T == 1024 and D == 4096) else 512
    SLOT = nk * WSLW
    wsl = nc.alloc_sbuf_tensor(f"{name}_wsl", [128, 2 * SLOT], BF16)
    psA = nc.alloc_psum_tensor(f"{name}_psA", [128, 1024], F32)
    psB = nc.alloc_psum_tensor(f"{name}_psB", [128, 1024], F32)
    psR = nc.alloc_psum_tensor(f"{name}_psR", [128, 1024], F32)
    KA, KB, KR = (name + "psA",), (name + "psB",), (name + "psR",)
    cosT = nc.alloc_sbuf_tensor(f"{name}_cos", [128, T], BF16)
    sinT = nc.alloc_sbuf_tensor(f"{name}_sin", [128, T], BF16)
    rsign = nc.alloc_sbuf_tensor(f"{name}_rs", [128, 128], BF16)
    qsb = nc.alloc_sbuf_tensor(f"{name}_qs", [128, 2 * T], BF16)
    rsb = nc.alloc_sbuf_tensor(f"{name}_rb", [128, 2 * T], BF16)
    qro = nc.alloc_sbuf_tensor(f"{name}_qr", [128, 2 * T], BF16)
    winT = nc.alloc_sbuf_tensor(f"{name}_wi", [128, nk * H], BF16)
    stg = nc.alloc_sbuf_tensor(f"{name}_stg", [128, 3 * 512], F32)
    vstg = nc.alloc_sbuf_tensor(f"{name}_vsg", [128, 3 * 512], BF16)
    dts = nc.alloc_sbuf_tensor(f"{name}_dts", [H, T], F32)

    for d_, s_ in ((cosT_dram, cosT), (sinT_dram, sinT), (rsign_dram, rsign)):
        ctx.dma("gpsimd", lambda e, d_=d_, s_=s_: e.dma_start(
            out=s_[:, :], in_=d_[:, :]),
            reads=[], writes=[(name + "cst", id(s_))], tag=f"{name}_ldc")
    ckeys = [(name + "cst", id(s_)) for s_ in (cosT, sinT, rsign)]
    src2 = winT_dram[:, :].rearrange("(kt p) c -> p kt c", p=128)
    ctx.dma("gpsimd", lambda e: e.dma_start(
        out=winT[:, :].rearrange("p (kt c) -> p kt c", c=H), in_=src2),
        reads=[], writes=[(name + "wi", 0)], tag=f"{name}_ldwi")

    norm = Norm(ctx, name + "n", lambda tt: x2_dram[tt * 128:(tt + 1) * 128, :],
                T, D, nslots=2)
    emit_xlnT(ctx, name, norm, xlnT, FxT, nk, T, ident, psA, psB)

    for which, wd, nhead, outd in (("q", wqT_dram, H, qT_dram),
                                   ("k", wkT_dram, KVH, kT_dram)):
        for cg in range(cdiv(nhead * 128, WSLW)):
            c0 = cg * WSLW
            gw = min(WSLW, nhead * 128 - c0)
            slot = cg % 2
            sl = wsl[:, slot * SLOT: slot * SLOT + nk * gw]
            src = wd[:, c0:c0 + gw].rearrange("(kt p) c -> p kt c", p=128)
            ctx.dma("gpsimd", lambda e, sl=sl, src=src, gw=gw: e.dma_start(
                out=sl.rearrange("p (kt c) -> p kt c", c=gw), in_=src),
                reads=[], writes=[(name + "wsl", slot)],
                tag=f"{name}_ld{which}{cg}")
            for ii in range(gw // 128):
                hd = (c0 + ii * 128) // 128
                pp, psk = (psA, KA) if hd % 2 == 0 else (psB, KB)
                for kt in range(nk):
                    for h2 in range(cdiv(T, 512)):
                        hw_ = min(512, T - h2 * 512)
                        ctx.op("tensor", lambda e, pp=pp, kt=kt, h2=h2, sl=sl, ii=ii, gw=gw, hw_=hw_: e.matmul(
                            pp[:, h2 * 512: h2 * 512 + hw_],
                            sl[:, kt * gw + ii * 128: kt * gw + ii * 128 + 128],
                            xlnT[:, kt * T + h2 * 512: kt * T + h2 * 512 + hw_],
                            start=(kt == 0), stop=(kt == nk - 1)),
                            reads=[(name + "wsl", slot)] + [
                                ("xlnT" + name, kt, h2 * 4 + j)
                                for j in range(hw_ // 128)],
                            writes=[psk], tag=f"{name}_{which}mm{hd}_{kt}_{h2}")
                sl_ = hd % 2
                qs_ = qsb[:, sl_ * T:(sl_ + 1) * T]
                ctx.op("scalar", lambda e, pp=pp, qs_=qs_: e.activation(
                    out=qs_, in_=pp[:, :T], func=AF.Copy),
                    reads=[psk], writes=[(name + "qsb", sl_)],
                    tag=f"{name}_{which}ev{hd}")
                for h2 in range(cdiv(T, 512)):
                    hw_ = min(512, T - h2 * 512)
                    ctx.op("tensor", lambda e, qs_=qs_, h2=h2, hw_=hw_: e.matmul(
                        psR[:, h2 * 512:h2 * 512 + hw_], rsign[:, :],
                        qs_[:, h2 * 512:h2 * 512 + hw_], start=True, stop=True),
                        reads=[(name + "qsb", sl_)] + ckeys, writes=[KR],
                        tag=f"{name}_{which}rt{hd}_{h2}")
                rs_ = rsb[:, sl_ * T:(sl_ + 1) * T]
                ctx.op("scalar", lambda e, rs_=rs_: e.activation(
                    out=rs_, in_=psR[:, :T], func=AF.Copy),
                    reads=[KR], writes=[(name + "rsb", sl_)],
                    tag=f"{name}_{which}re{hd}")
                qr_ = qro[:, sl_ * T:(sl_ + 1) * T]
                ctx.op("vector", lambda e, rs_=rs_, qr_=qr_: e.tensor_mul(
                    out=qr_, in0=rs_, in1=sinT[:, :]),
                    reads=[(name + "rsb", sl_)] + ckeys, writes=[(name + "qro", sl_)],
                    tag=f"{name}_{which}rm{hd}")
                ctx.op("vector", lambda e, qs_=qs_, rs_=rs_: e.tensor_mul(
                    out=rs_, in0=qs_, in1=cosT[:, :]),
                    reads=[(name + "qsb", sl_)] + ckeys, writes=[(name + "rsb", sl_)],
                    tag=f"{name}_{which}cm{hd}")
                ctx.op("vector", lambda e, qr_=qr_, rs_=rs_: e.tensor_add(
                    out=qr_, in0=qr_, in1=rs_),
                    reads=[(name + "qro", sl_), (name + "rsb", sl_)],
                    writes=[(name + "qro", sl_)], tag=f"{name}_{which}ra{hd}")
                ctx.dma("sync", lambda e, qr_=qr_, outd=outd, hd=hd: e.dma_start(
                    out=outd[hd * 128:(hd + 1) * 128, :], in_=qr_),
                    reads=[(name + "qro", sl_)], writes=[(name + "qst", sl_)],
                    tag=f"{name}_{which}st{hd}")

    for which, wd, cols, outd in (("v", wvT_dram, KVH * 128, v_dram),
                                  ("g", wgT_dram, H * 128, sg_dram)):
        for cg in range(cdiv(cols, WSLW)):
            gw = min(WSLW, cols - cg * WSLW)
            slot = cg % 2
            sl = wsl[:, slot * SLOT: slot * SLOT + nk * gw]
            src = wd[:, cg * WSLW:cg * WSLW + gw].rearrange(
                "(kt p) c -> p kt c", p=128)
            ctx.dma("gpsimd", lambda e, sl=sl, src=src, gw=gw: e.dma_start(
                out=sl.rearrange("p (kt c) -> p kt c", c=gw), in_=src),
                reads=[], writes=[(name + "wsl", slot)],
                tag=f"{name}_ld{which}2{cg}")
            for tt in range(nt):
                pp, psk = (psA, KA) if tt % 2 == 0 else (psB, KB)
                for kt in range(nk):
                    ctx.op("tensor", lambda e, pp=pp, kt=kt, tt=tt, sl=sl, gw=gw: e.matmul(
                        pp[:, :gw],
                        xlnT[:, kt * T + tt * 128: kt * T + tt * 128 + 128],
                        sl[:, kt * gw: kt * gw + gw],
                        start=(kt == 0), stop=(kt == nk - 1)),
                        reads=[(name + "wsl", slot), ("xlnT" + name, kt, tt)],
                        writes=[psk], tag=f"{name}_{which}3{cg}_{tt}_{kt}")
                si = (cg * nt + tt) % 3
                ss = stg[:, si * 512:si * 512 + gw]
                if which == "v":
                    ssb = vstg[:, si * 512:si * 512 + gw]
                    ctx.op("scalar", lambda e, pp=pp, ssb=ssb, gw=gw: e.activation(
                        out=ssb, in_=pp[:, :gw], func=AF.Copy),
                        reads=[psk], writes=[(name + "stg", si)],
                        tag=f"{name}_vev{cg}_{tt}")
                    ctx.dma("sync", lambda e, ssb=ssb, outd=outd, tt=tt, cg=cg, gw=gw: e.dma_start(
                        out=outd[tt * 128:(tt + 1) * 128,
                                 cg * WSLW:cg * WSLW + gw], in_=ssb),
                        reads=[(name + "stg", si)],
                        writes=[(name + "vst", (cg * nt + tt) % 8)],
                        tag=f"{name}_vst{cg}_{tt}")
                else:
                    ctx.op("scalar", lambda e, pp=pp, ss=ss, gw=gw: e.activation(
                        out=ss, in_=pp[:, :gw], func=AF.Silu),
                        reads=[psk], writes=[(name + "stg", si)],
                        tag=f"{name}_gev{cg}_{tt}")
                    ctx.dma("sync", lambda e, ss=ss, outd=outd, tt=tt, cg=cg, gw=gw: e.dma_start(
                        out=outd[tt * 128:(tt + 1) * 128,
                                 cg * WSLW:cg * WSLW + gw], in_=ss),
                        reads=[(name + "stg", si)],
                        writes=[(name + "gst", (cg * nt + tt) % 8)],
                        tag=f"{name}_gst{cg}_{tt}")
    for kt in range(nk):
        for h2 in range(cdiv(T, 512)):
            hw_ = min(512, T - h2 * 512)
            ctx.op("tensor", lambda e, kt=kt, h2=h2, hw_=hw_: e.matmul(
                psR[:H, h2 * 512:h2 * 512 + hw_], winT[:, kt * H:(kt + 1) * H],
                xlnT[:, kt * T + h2 * 512: kt * T + h2 * 512 + hw_],
                start=(kt == 0), stop=(kt == nk - 1)),
                reads=[(name + "wi", 0)] + [("xlnT" + name, kt, h2 * 4 + j)
                                            for j in range(hw_ // 128)],
                writes=[KR], tag=f"{name}_dt{kt}_{h2}")
    ctx.op("vector", lambda e: e.tensor_copy(out=dts[:, :], in_=psR[:H, :T]),
           reads=[KR], writes=[(name + "dts", 0)], tag=f"{name}_dte")
    ctx.dma("sync", lambda e: e.dma_start(out=dtl_dram[:, :], in_=dts[:, :]),
            reads=[(name + "dts", 0)], writes=[(name + "dst", 0)],
            tag=f"{name}_dst")
    return ([(name + "qst", i) for i in range(2)] +
            [(name + "vst", i) for i in range(8)] +
            [(name + "gst", i) for i in range(8)] + [(name + "dst", 0)])



def emit_l1_scan(ctx, qT_dram, kT_dram, v_dram, dtl_dram, dtb_dram,
                 alogb_dram, y_dram, ident, identF, causalT_dram,
                 L=8192, QH=4, name="t0"):
    nc = ctx.nc
    NC = L // 128
    Dh = 128
    W = QH * Dh
    qT = nc.alloc_sbuf_tensor(f"{name}_qT", [128, QH * L], BF16)
    kT = nc.alloc_sbuf_tensor(f"{name}_kT", [128, L], BF16)
    vtl = nc.alloc_sbuf_tensor(f"{name}_v", [128, L], BF16)
    DTt = nc.alloc_sbuf_tensor(f"{name}_DT", [QH, L], F32)
    Gt = nc.alloc_sbuf_tensor(f"{name}_G", [QH, L], F32)
    sfs = nc.alloc_sbuf_tensor(f"{name}_sf", [QH, 512], F32)
    dtb = nc.alloc_sbuf_tensor(f"{name}_dtb", [QH, 1], F32)
    alog = nc.alloc_sbuf_tensor(f"{name}_al", [QH, 2], F32)
    zs128 = nc.alloc_sbuf_tensor(f"{name}_z", [QH, 128], F32)
    onesF = nc.alloc_sbuf_tensor(f"{name}_of", [1, 128], F32)
    causT = nc.alloc_sbuf_tensor(f"{name}_ca", [128, 128], BF16)
    grow = nc.alloc_sbuf_tensor(f"{name}_gr", [1, 2 * QH * 128], F32)
    grow2 = nc.alloc_sbuf_tensor(f"{name}_gr2", [1, QH * (L // 128)], F32)
    gam = nc.alloc_sbuf_tensor(f"{name}_gam", [128, QH * NC], F32)
    lamT = nc.alloc_sbuf_tensor(f"{name}_lT", [128, 2 * QH], F32)
    eT = nc.alloc_sbuf_tensor(f"{name}_eT", [128, 2 * QH], F32)
    wT = nc.alloc_sbuf_tensor(f"{name}_wT", [128, 2 * QH], F32)
    lec = nc.alloc_sbuf_tensor(f"{name}_lec", [QH, 512], F32)
    mm_ = nc.alloc_sbuf_tensor(f"{name}_mm", [128, 128], BF16)
    ktok = nc.alloc_sbuf_tensor(f"{name}_ktk", [128, 128], BF16)
    warg = nc.alloc_sbuf_tensor(f"{name}_wa", [128, 128], F32)
    wdec = nc.alloc_sbuf_tensor(f"{name}_wd", [128, 128], BF16)
    wfin = nc.alloc_sbuf_tensor(f"{name}_wf", [128, 2 * QH * 128], BF16)
    ktl = nc.alloc_sbuf_tensor(f"{name}_ktl", [128, 2 * QH * 128], BF16)
    H_ = nc.alloc_sbuf_tensor(f"{name}_H", [128, W], F32)
    Hb = nc.alloc_sbuf_tensor(f"{name}_Hb", [128, W], BF16)
    ysb = nc.alloc_sbuf_tensor(f"{name}_ysb", [128, 2 * W], F32)
    sqd = nc.alloc_sbuf_tensor(f"{name}_sqd", [128, 128], BF16)
    sst = nc.alloc_sbuf_tensor(f"{name}_sst", [128, 4], F32)
    psA = nc.alloc_psum_tensor(f"{name}_psA", [128, 512], F32)
    psB = nc.alloc_psum_tensor(f"{name}_psB", [128, 512], F32)
    psC = nc.alloc_psum_tensor(f"{name}_psC", [128, 512], F32)
    psD = nc.alloc_psum_tensor(f"{name}_psD", [128, 512], F32)
    psS = nc.alloc_psum_tensor(f"{name}_psS", [128, 512], F32)
    KA, KB, KC, KD, KS = ((name + "A",), (name + "B",), (name + "C",),
                          (name + "D",), (name + "S",))

    ctx.dma("sync", lambda e: e.dma_start(
        out=qT[:, :].rearrange("p (q c) -> p q c", c=L),
        in_=qT_dram[:, :].rearrange("(q p) c -> p q c", p=128)),
        reads=[], writes=[(name + "qT", 0)], tag=f"{name}_ldq")
    ctx.dma("sync", lambda e: e.dma_start(out=kT[:, :], in_=kT_dram[:, :]),
            reads=[], writes=[(name + "kT", 0)], tag=f"{name}_ldk")
    ctx.dma("sync", lambda e: e.dma_start(
        out=vtl[:, :].rearrange("p (c d) -> p c d", d=128),
        in_=v_dram[:, :].rearrange("(c p) d -> p c d", p=128)),
        reads=[], writes=[(name + "v", 0)], tag=f"{name}_ldv")
    ctx.dma("gpsimd", lambda e: e.dma_start(out=DTt[:, :], in_=dtl_dram[:, :]),
            reads=[], writes=[(name + "dtr", 0)], tag=f"{name}_lddt")
    for d_, s_, kk in ((dtb_dram, dtb, "dtb"), (alogb_dram, alog, "al")):
        ctx.dma("gpsimd", lambda e, d_=d_, s_=s_: e.dma_start(
            out=s_[:, :1], in_=d_[:, :]),
            reads=[], writes=[(name + kk, 0)], tag=f"{name}_ld{kk}")
    ctx.dma("gpsimd", lambda e: e.dma_start(out=causT[:, :], in_=causalT_dram[:, :]),
            reads=[], writes=[(name + "ca", 0)], tag=f"{name}_ldca")
    ctx.op("gpsimd", lambda e: e.memset(onesF[:, :], 1.0), reads=[],
           writes=[(name + "onesF", 0)], tag=f"{name}_onf")
    ctx.op("gpsimd", lambda e: e.memset(Hb[:, :], 0.0), reads=[],
           writes=[(name + "Hb", 0)], tag=f"{name}_hb0")
    ctx.op("gpsimd", lambda e: e.memset(zs128[:, :], 0.0), reads=[],
           writes=[(name + "z0", 0)], tag=f"{name}_z0")
    ctx.op("gpsimd", lambda e: e.memset(H_[:, :], 0.0), reads=[],
           writes=[(name + "H", 0)], tag=f"{name}_h0")
    one_ap, one_key = ctx.constf(1.0)
    eps_ap, eps_key = ctx.constf(1e-5)
    ctx.op("scalar", lambda e: e.activation(out=alog[:, 1:2], in_=alog[:, :1],
                                            func=AF.Exp),
           reads=[(name + "al", 0)], writes=[(name + "A0", 0)], tag=f"{name}_ea")
    ctx.op("vector", lambda e: e.tensor_scalar(
        out=alog[:, 1:2], in0=alog[:, 1:2], scalar1=-1.0, scalar2=None,
        op0=ALU.mult, op1=ALU.bypass),
        reads=[(name + "A0", 0)], writes=[(name + "A", 0)], tag=f"{name}_na")
    gkeys = []
    for c in range(NC):
        dsl = DTt[:, c * 128:(c + 1) * 128]
        a_, b_ = sfs[:, 0:128], sfs[:, 128:256]
        ctx.op("scalar", lambda e, dsl=dsl, a_=a_: e.activation(
            out=a_, in_=dsl, func=AF.Abs, bias=dtb[:, :1]),
            reads=[(name + "dtr", 0), (name + "dtb", 0)],
            writes=[(name + "sa", 0)], tag=f"{name}_ab{c}")
        ctx.op("scalar", lambda e, a_=a_: e.activation(
            out=a_, in_=a_, func=AF.Exp, scale=-1.0),
            reads=[(name + "sa", 0)], writes=[(name + "sa", 0)], tag=f"{name}_ex{c}")
        ctx.op("scalar", lambda e, a_=a_: e.activation(
            out=a_, in_=a_, func=AF.Ln, bias=one_ap[:QH, :1]),
            reads=[(name + "sa", 0), one_key], writes=[(name + "sa", 0)],
            tag=f"{name}_ln{c}")
        ctx.op("scalar", lambda e, dsl=dsl, b_=b_: e.activation(
            out=b_, in_=dsl, func=AF.Relu, bias=dtb[:, :1]),
            reads=[(name + "dtr", 0), (name + "dtb", 0)],
            writes=[(name + "sb", 0)], tag=f"{name}_rl{c}")
        ctx.op("vector", lambda e, dsl=dsl, a_=a_, b_=b_: e.tensor_add(
            out=dsl, in0=a_, in1=b_),
            reads=[(name + "sa", 0), (name + "sb", 0)], writes=[(name + "dt", c)],
            tag=f"{name}_sp{c}")
        ctx.op("vector", lambda e, dsl=dsl, a_=a_: e.tensor_scalar(
            out=a_, in0=dsl, scalar1=alog[:, 1:2], scalar2=None, op0=ALU.mult,
            op1=ALU.bypass),
            reads=[(name + "dt", c), (name + "A", 0)], writes=[(name + "sa", 0)],
            tag=f"{name}_g{c}")
        ctx.op("vector", lambda e, c=c, a_=a_: e.tensor_tensor_scan(
            out=Gt[:, c * 128:(c + 1) * 128], data0=a_,
            data1=zs128[:, :], initial=0.0, op0=ALU.add, op1=ALU.add),
            reads=[(name + "sa", 0), (name + "z0", 0)], writes=[(name + "G", c)],
            tag=f"{name}_cs{c}")
        gkeys.append((name + "G", c))
    dkeys = [(name + "dt", c) for c in range(NC)]
    ctx.op("vector", lambda e: e.tensor_copy(
        out=lec[:, 0:NC], in_=bass.AP(Gt, 127, [[L, QH], [128, NC]])),
        reads=gkeys, writes=[(name + "ge", 0)], tag=f"{name}_ge")
    ctx.op("scalar", lambda e: e.activation(out=lec[:, NC:2 * NC], in_=lec[:, 0:NC],
                                            func=AF.Exp),
           reads=[(name + "ge", 0)], writes=[(name + "gam0", 0)], tag=f"{name}_gx")
    ctx.dma("gpsimd", lambda e: e.dma_start(
        out=grow2[:1, :QH * NC], in_=lec[:, NC:2 * NC]),
        reads=[(name + "gam0", 0)], writes=[(name + "grow2", 0)],
        tag=f"{name}_gf")
    ctx.op("tensor", lambda e: e.matmul(
        psD[:, :QH * NC], onesF[:1, :], grow2[:1, :QH * NC],
        start=True, stop=True),
        reads=[(name + "grow2", 0), (name + "onesF", 0)], writes=[KD],
        tag=f"{name}_gb")
    ctx.op("vector", lambda e: e.tensor_copy(out=gam[:, :], in_=psD[:, :QH * NC]),
           reads=[KD], writes=[(name + "gam", 0)], tag=f"{name}_gc")

    for c in range(NC):
        sl = c % 2
        gr = grow[:1, sl * QH * 128:(sl + 1) * QH * 128]
        ctx.dma("gpsimd", lambda e, gr=gr, c=c: e.dma_start(
            out=gr, in_=Gt[:, c * 128:(c + 1) * 128]),
            reads=gkeys, writes=[(name + "gr", sl)], tag=f"{name}_gr{c}")
        ctx.op("scalar", lambda e, c=c: e.activation(
            out=lec[:, 256:384], in_=DTt[:, c * 128:(c + 1) * 128], func=AF.Ln),
            reads=dkeys, writes=[(name + "lec", 0)], tag=f"{name}_lc{c}")
        ctx.op("vector", lambda e, c=c: e.tensor_sub(
            out=lec[:, 256:384], in0=lec[:, 256:384],
            in1=Gt[:, c * 128:(c + 1) * 128]),
            reads=[(name + "lec", 0)] + gkeys, writes=[(name + "lec", 0)],
            tag=f"{name}_lc2{c}")
        ctx.op("scalar", lambda e, c=c: e.activation(
            out=lec[:, 384:512], in_=Gt[:, c * 128:(c + 1) * 128], func=AF.Exp),
            reads=gkeys, writes=[(name + "lec", 1)], tag=f"{name}_ec{c}")
        for (off_, dstb, kk) in ((256, lamT, "lT"), (384, eT, "eT")):
            ctx.op("tensor", lambda e, off_=off_: e.matmul(
                psD[:, :QH], lec[:, off_:off_ + 128], identF[:QH, :QH],
                start=True, stop=True),
                reads=[(name + "lec", 0), (name + "lec", 1), "identF"],
                writes=[KD], tag=f"{name}_{kk}t{c}")
            ctx.op("vector", lambda e, dstb=dstb, sl=sl: e.tensor_copy(
                out=dstb[:, sl * QH:(sl + 1) * QH], in_=psD[:, :QH]),
                reads=[KD], writes=[(name + kk, sl)], tag=f"{name}_{kk}e{c}")
        ctx.op("scalar", lambda e, c=c: e.activation(
            out=lec[:, 384:512], in_=lec[:, 256:384], func=AF.Exp,
            bias=bass.AP(Gt, c * 128 + 127, [[L, QH], [1, 1]])),
            reads=[(name + "lec", 0), (name + "eT", sl)] + gkeys,
            writes=[(name + "lec", 1)], tag=f"{name}_wf{c}")
        ctx.op("tensor", lambda e: e.matmul(
            psD[:, :QH], lec[:, 384:512], identF[:QH, :QH], start=True, stop=True),
            reads=[(name + "lec", 1), "identF"], writes=[KD], tag=f"{name}_wft{c}")
        ctx.op("vector", lambda e, sl=sl: e.tensor_copy(
            out=wT[:, sl * QH:(sl + 1) * QH], in_=psD[:, :QH]),
            reads=[KD], writes=[(name + "wT", sl)], tag=f"{name}_wfe{c}")
        ctx.op("tensor", lambda e, c=c: e.matmul(
            psC[:, 128:256], kT[:, c * 128:(c + 1) * 128], ident[:, :],
            start=True, stop=True),
            reads=[(name + "kT", 0), "ident"], writes=[KC], tag=f"{name}_kt{c}")
        ctx.op("vector", lambda e: e.tensor_copy(
            out=ktok[:, :], in_=psC[:, 128:256]),
            reads=[KC], writes=[(name + "ktk", 0)], tag=f"{name}_kte{c}")
        for qh in range(QH):
            ctx.op("tensor", lambda e, c=c, qh=qh: e.matmul(
                psC[:, :128], kT[:, c * 128:(c + 1) * 128],
                qT[:, qh * L + c * 128: qh * L + (c + 1) * 128],
                start=True, stop=True),
                reads=[(name + "kT", 0), (name + "qT", 0)], writes=[KC],
                tag=f"{name}_M{c}_{qh}")
            ctx.op("vector", lambda e: e.tensor_mul(
                out=mm_[:, :], in0=psC[:, :128], in1=causT[:, :]),
                reads=[KC, (name + "ca", 0)], writes=[(name + "mm", 0)],
                tag=f"{name}_Mm{c}_{qh}")
            ctx.op("tensor", lambda e, gr=gr, qh=qh: e.matmul(
                psD[:, :128], onesF[:1, :], gr[:1, qh * 128:(qh + 1) * 128],
                start=True, stop=True),
                reads=[(name + "gr", sl), (name + "onesF", 0)], writes=[KD],
                tag=f"{name}_gb{c}_{qh}")
            ctx.op("vector", lambda e, qh=qh, sl=sl: e.tensor_scalar(
                out=warg[:, :], in0=psD[:, :128],
                scalar1=lamT[:, sl * QH + qh: sl * QH + qh + 1], scalar2=15.0,
                op0=ALU.add, op1=ALU.min),
                reads=[KD, (name + "lT", sl)], writes=[(name + "wa", 0)],
                tag=f"{name}_wa{c}_{qh}")
            ctx.op("scalar", lambda e: e.activation(
                out=wdec[:, :], in_=warg[:, :], func=AF.Exp),
                reads=[(name + "wa", 0)], writes=[(name + "wd", 0)],
                tag=f"{name}_wx{c}_{qh}")
            wf_ = wfin[:, (sl * QH + qh) * 128:(sl * QH + qh + 1) * 128]
            ctx.op("vector", lambda e, wf_=wf_: e.tensor_mul(
                out=wf_, in0=mm_[:, :], in1=wdec[:, :]),
                reads=[(name + "wd", 0), (name + "mm", 0)],
                writes=[(name + "wfin", sl, qh)], tag=f"{name}_wm{c}_{qh}")
            kt_ = ktl[:, (sl * QH + qh) * 128:(sl * QH + qh + 1) * 128]
            ctx.op("vector", lambda e, kt_=kt_, qh=qh, sl=sl: e.tensor_scalar(
                out=kt_, in0=ktok[:, :], scalar1=wT[:, sl * QH + qh: sl * QH + qh + 1],
                scalar2=None, op0=ALU.mult, op1=ALU.bypass),
                reads=[(name + "ktk", 0), (name + "wT", sl)],
                writes=[(name + "ktl", sl, qh)], tag=f"{name}_ktl{c}_{qh}")
        for j in range(QH):
            ctx.op("tensor", lambda e, j=j, sl=sl, c=c: e.matmul(
                psA[:, j * 128:(j + 1) * 128],
                wfin[:, (sl * QH + j) * 128:(sl * QH + j + 1) * 128],
                vtl[:, c * 128:(c + 1) * 128],
                start=(j == 0), stop=(j == QH - 1)),
                reads=[(name + "wfin", sl, j), (name + "v", 0)], writes=[KA],
                tag=f"{name}_y1{c}_{j}")
        for j in range(QH):
            ctx.op("tensor", lambda e, j=j, c=c: e.matmul(
                psB[:, j * 128:(j + 1) * 128],
                qT[:, j * L + c * 128: j * L + (c + 1) * 128],
                Hb[:, j * 128:(j + 1) * 128],
                start=(j == 0), stop=(j == QH - 1)),
                reads=[(name + "qT", 0), (name + "Hb", 0)], writes=[KB],
                tag=f"{name}_y2{c}_{j}")
        for j in range(QH):
            ctx.op("tensor", lambda e, j=j, sl=sl, c=c: e.matmul(
                psS[:, j * 128:(j + 1) * 128],
                ktl[:, (sl * QH + j) * 128:(sl * QH + j + 1) * 128],
                vtl[:, c * 128:(c + 1) * 128],
                start=(j == 0), stop=(j == QH - 1)),
                reads=[(name + "ktl", sl, j), (name + "v", 0)], writes=[KS],
                tag=f"{name}_sc{c}_{j}")
        for j in range(QH):
            ys_ = ysb[:, sl * W + j * 128: sl * W + (j + 1) * 128]
            ctx.op("vector", lambda e, ys_=ys_, j=j, sl=sl: e.tensor_scalar(
                out=ys_, in0=psB[:, j * 128:(j + 1) * 128],
                scalar1=eT[:, sl * QH + j: sl * QH + j + 1], scalar2=None,
                op0=ALU.mult, op1=ALU.bypass),
                reads=[KB, (name + "eT", sl)], writes=[(name + "ysb", sl, j)],
                tag=f"{name}_ye{c}_{j}")
            ctx.op("vector", lambda e, ys_=ys_, j=j: e.tensor_add(
                out=ys_, in0=psA[:, j * 128:(j + 1) * 128], in1=ys_),
                reads=[KA, (name + "ysb", sl, j)], writes=[(name + "ysb", sl, j)],
                tag=f"{name}_ya{c}_{j}")
            ctx.op("scalar", lambda e, ys_=ys_, j=j: e.activation(
                out=sqd[:, :], in_=ys_, func=AF.Square,
                accum_out=sst[:, j:j + 1]),
                reads=[(name + "ysb", sl, j)], writes=[(name + "sq", 0), (name, "ss", j)],
                tag=f"{name}_sq{c}_{j}")
            ctx.op("scalar", lambda e, j=j: e.activation(
                out=sst[:, j:j + 1], in_=sst[:, j:j + 1], func=AF.Ln,
                bias=eps_ap[:, :1], scale=1.0 / Dh),
                reads=[(name, "ss", j), eps_key], writes=[(name, "ss", j)],
                tag=f"{name}_sl{c}_{j}")
            ctx.op("scalar", lambda e, j=j: e.activation(
                out=sst[:, j:j + 1], in_=sst[:, j:j + 1], func=AF.Exp,
                scale=-0.5),
                reads=[(name, "ss", j)], writes=[(name, "ss", j)],
                tag=f"{name}_se{c}_{j}")
            ctx.op("scalar", lambda e, ys_=ys_, j=j: e.activation(
                out=ys_, in_=ys_, func=AF.Identity, scale=sst[:, j:j + 1]),
                reads=[(name, "ss", j), (name + "ysb", sl, j)],
                writes=[(name + "ysb", sl, j)], tag=f"{name}_sa{c}_{j}")
        gbc = sap(gam, QH * NC, c, [[NC, QH], [0, Dh]])
        ctx.op("vector", lambda e, gbc=gbc: e.tensor_mul(
            out=H_[:, :], in0=H_[:, :], in1=gbc),
            reads=[(name + "H", 0), (name + "gam", 0)], writes=[(name + "H", 0)],
            tag=f"{name}_hg{c}")
        ctx.op("vector", lambda e: e.tensor_add(
            out=H_[:, :], in0=H_[:, :], in1=psS[:, :QH * 128]),
            reads=[(name + "H", 0), KS], writes=[(name + "H", 0)],
            tag=f"{name}_hs{c}")
        ctx.op("scalar", lambda e: e.copy(out=Hb[:, :], in_=H_[:, :]),
               reads=[(name + "H", 0)], writes=[(name + "Hb", 0)],
               tag=f"{name}_hb{c}")
        ysl = ysb[:, sl * W:(sl + 1) * W]
        ctx.dma("scalar", lambda e, ysl=ysl, c=c: e.dma_start(
            out=y_dram[c * 128:(c + 1) * 128, :], in_=ysl),
            reads=[(name + "ysb", sl, j2) for j2 in range(QH)],
            writes=[(name + "yd", sl)], tag=f"{name}_sy{c}")
    return [(name + "yd", 0), (name + "yd", 1)]


"""Driver: kernel(**inputs) running the 9-launch unfused checkpoint."""
import numpy as np
import ml_dtypes
import concourse.bass as bass
import concourse.mybir as mybir
from concourse.bass_utils import run_bass_kernel_spmd

BFD = ml_dtypes.bfloat16
NCORE = 8
T = 1024
D = 4096
L = 8192


def _rope_tabs(Tn, t0):
    Dh = 128
    inv = 1.0 / (10000.0 ** (np.arange(0, Dh, 2) / Dh))
    ang = (np.arange(t0, t0 + Tn)[:, None] * inv[None, :])
    cos = np.concatenate([np.cos(ang), np.cos(ang)], -1)
    sin = np.concatenate([np.sin(ang), np.sin(ang)], -1)
    return (np.ascontiguousarray(cos.T.astype(np.float32)),
            np.ascontiguousarray(sin.T.astype(np.float32)))


def _rsign():
    R = np.zeros((128, 128), np.float32)
    for d in range(64):
        R[d + 64, d] = -1.0
    for d in range(64, 128):
        R[d - 64, d] = 1.0
    return R


def _b_dq():
    nc = bass.Bass("TRN2", target_bir_lowering=False)
    qg = nc.dram_tensor("qg", [11008, 256], I32, kind="ExternalInput")
    sg_ = nc.dram_tensor("sg", [11008, 8], F32, kind="ExternalInput")
    zg = nc.dram_tensor("zg", [11008, 8], F32, kind="ExternalInput")
    qu = nc.dram_tensor("qu", [11008, 256], I32, kind="ExternalInput")
    su = nc.dram_tensor("su", [11008, 8], F32, kind="ExternalInput")
    zu = nc.dram_tensor("zu", [11008, 8], F32, kind="ExternalInput")
    qd = nc.dram_tensor("qd", [4096, 704], I32, kind="ExternalInput")
    sd = nc.dram_tensor("sd", [4096, 22], F32, kind="ExternalInput")
    zd = nc.dram_tensor("zd", [4096, 22], F32, kind="ExternalInput")
    idd = nc.dram_tensor("idd", [128, 128], F32, kind="ExternalInput")
    og = nc.dram_tensor("og", [512, 11008], BF16, kind="ExternalOutput")
    ou = nc.dram_tensor("ou", [512, 11008], BF16, kind="ExternalOutput")
    od = nc.dram_tensor("od", [1408, 4096], BF16, kind="ExternalOutput")
    ctx = Ctx(nc)
    ident = emit_ident(ctx, idd)
    ps = [nc.alloc_psum_tensor(f"dqps{i}", [128, 1024], F32) for i in range(2)]
    fins = []
    fins += emit_dequant_slice(ctx, "dg", qg, sg_, zg, og, 11008, 256, ident,
                               CH=8, ps=ps, pskn="dq")
    fins += emit_dequant_slice(ctx, "du", qu, su, zu, ou, 11008, 256, ident,
                               CH=8, ps=ps, pskn="dq")
    fins += emit_dequant_slice(ctx, "dd", qd, sd, zd, od, 4096, 704, ident,
                               CH=4, ps=ps, pskn="dq")
    ctx.op("sync", lambda e: e.nop(), reads=fins, writes=[], tag="fin")
    ctx.finalize_and_emit()
    return nc


def _b_l0a():
    PROJ = 2 * D + 256 + 64
    nc = bass.Bass("TRN2", target_bir_lowering=False)
    xh = nc.dram_tensor("xh", [T + 3, D], F32, kind="ExternalInput")
    inpw = nc.dram_tensor("inpw", [D, PROJ], F32, kind="ExternalInput")
    lnw = nc.dram_tensor("lnw", [D], F32, kind="ExternalInput")
    mup = nc.dram_tensor("mup", [PROJ], F32, kind="ExternalInput")
    convw = nc.dram_tensor("convw", [D + 256, 4], F32, kind="ExternalInput")
    idd = nc.dram_tensor("idd", [128, 128], F32, kind="ExternalInput")
    sz = nc.dram_tensor("sz", [T, D], F32, kind="ExternalOutput")
    xs = nc.dram_tensor("xs", [T, D], BF16, kind="ExternalOutput")
    bT = nc.dram_tensor("bT", [128, T], BF16, kind="ExternalOutput")
    cT = nc.dram_tensor("cT", [128, T], BF16, kind="ExternalOutput")
    dtr = nc.dram_tensor("dtr", [64, T], F32, kind="ExternalOutput")
    ctx = Ctx(nc)
    ident = emit_ident(ctx, idd)
    identF = nc.alloc_sbuf_tensor("identF", [128, 128], F32)
    ctx.dma("gpsimd", lambda e: e.dma_start(out=identF[:, :], in_=idd[:, :]),
            reads=[], writes=["identF"], tag="ldidF")
    fins = emit_l0_attn_a(ctx, xh, inpw, lnw, mup, convw, sz, xs, bT, cT,
                          dtr, ident, identF, T=T, D=D, NH=64)
    ctx.op("sync", lambda e: e.nop(), reads=fins, writes=[], tag="fin")
    ctx.finalize_and_emit()
    return nc


def _b_l0s():
    NHH, HD = 8, 64
    nc = bass.Bass("TRN2", target_bir_lowering=False)
    xs = nc.dram_tensor("xs", [L, NHH * HD], BF16, kind="ExternalInput")
    bT = nc.dram_tensor("bT", [128, L], BF16, kind="ExternalInput")
    cT = nc.dram_tensor("cT", [128, L], BF16, kind="ExternalInput")
    dtr = nc.dram_tensor("dtr", [NHH, L], F32, kind="ExternalInput")
    dtb = nc.dram_tensor("dtb", [NHH, 1], F32, kind="ExternalInput")
    alog = nc.dram_tensor("alog", [NHH, 1], F32, kind="ExternalInput")
    drow = nc.dram_tensor("drow", [1, NHH], F32, kind="ExternalInput")
    idd = nc.dram_tensor("idd", [128, 128], F32, kind="ExternalInput")
    caus = nc.dram_tensor("caus", [128, 128], F32, kind="ExternalInput")
    y = nc.dram_tensor("y", [L, NHH * HD], F32, kind="ExternalOutput")
    ctx = Ctx(nc)
    ident = emit_ident(ctx, idd)
    identF = nc.alloc_sbuf_tensor("identF", [128, 128], F32)
    ctx.dma("gpsimd", lambda e: e.dma_start(out=identF[:, :], in_=idd[:, :]),
            reads=[], writes=["identF"], tag="ldidF")
    fins = emit_l0_scan(ctx, xs, bT, cT, dtr, dtb, alog, drow, y, ident,
                        identF, caus, L=L, NHH=NHH, HD=HD)
    ctx.op("sync", lambda e: e.nop(), reads=fins, writes=[], tag="fin")
    ctx.finalize_and_emit()
    return nc


def _b_gout(rms):
    nc = bass.Bass("TRN2", target_bir_lowering=False)
    y = nc.dram_tensor("y", [T, D], F32, kind="ExternalInput")
    g = nc.dram_tensor("g", [T, D], F32, kind="ExternalInput")
    wT = nc.dram_tensor("wT", [D, D], F32, kind="ExternalInput")
    fold = nc.dram_tensor("fold", [128, D // 128], F32, kind="ExternalInput")
    xres = nc.dram_tensor("xres", [T, D], F32, kind="ExternalInput")
    idd = nc.dram_tensor("idd", [128, 128], F32, kind="ExternalInput")
    xout = nc.dram_tensor("xout", [T, D], F32, kind="ExternalOutput")
    ctx = Ctx(nc)
    ident = emit_ident(ctx, idd)
    fins = emit_gate_outproj(ctx, y, g, wT, fold, xres, xout, ident,
                             T=T, D=D, rms=rms)
    ctx.op("sync", lambda e: e.nop(), reads=fins, writes=[], tag="fin")
    ctx.finalize_and_emit()
    return nc


def _b_ffn():
    nc = bass.Bass("TRN2", target_bir_lowering=False)
    x1 = nc.dram_tensor("x1", [T, D], F32, kind="ExternalInput")
    pg = nc.dram_tensor("pg", [D, 11008], BF16, kind="ExternalInput")
    pu = nc.dram_tensor("pu", [D, 11008], BF16, kind="ExternalInput")
    pd = nc.dram_tensor("pd", [11264, D], BF16, kind="ExternalInput")
    idd = nc.dram_tensor("idd", [128, 128], F32, kind="ExternalInput")
    x2 = nc.dram_tensor("x2", [T, D], F32, kind="ExternalOutput")
    ctx = Ctx(nc)
    ident = emit_ident(ctx, idd)
    fins = emit_l0_ffn(ctx, x1, pg, pu, pd, x2, ident, T=T, D=D, I=11008)
    ctx.op("sync", lambda e: e.nop(), reads=fins, writes=[], tag="fin")
    ctx.finalize_and_emit()
    return nc


def _b_l1a():
    H, KVH = 32, 8
    nc = bass.Bass("TRN2", target_bir_lowering=False)
    x2 = nc.dram_tensor("x2", [T, D], F32, kind="ExternalInput")
    wqT = nc.dram_tensor("wqT", [D, H * 128], F32, kind="ExternalInput")
    wkT = nc.dram_tensor("wkT", [D, KVH * 128], F32, kind="ExternalInput")
    wvT = nc.dram_tensor("wvT", [D, KVH * 128], F32, kind="ExternalInput")
    wgT = nc.dram_tensor("wgT", [D, H * 128], F32, kind="ExternalInput")
    winT = nc.dram_tensor("winT", [D, H], F32, kind="ExternalInput")
    cosT = nc.dram_tensor("cosT", [128, T], F32, kind="ExternalInput")
    sinT = nc.dram_tensor("sinT", [128, T], F32, kind="ExternalInput")
    rsg = nc.dram_tensor("rsg", [128, 128], F32, kind="ExternalInput")
    idd = nc.dram_tensor("idd", [128, 128], F32, kind="ExternalInput")
    qT = nc.dram_tensor("qT", [H * 128, T], BF16, kind="ExternalOutput")
    kT = nc.dram_tensor("kT", [KVH * 128, T], BF16, kind="ExternalOutput")
    v = nc.dram_tensor("v", [T, KVH * 128], BF16, kind="ExternalOutput")
    sg = nc.dram_tensor("sg", [T, H * 128], F32, kind="ExternalOutput")
    dtl = nc.dram_tensor("dtl", [H, T], F32, kind="ExternalOutput")
    ctx = Ctx(nc)
    ident = emit_ident(ctx, idd)
    identF = nc.alloc_sbuf_tensor("identF", [128, 128], F32)
    ctx.dma("gpsimd", lambda e: e.dma_start(out=identF[:, :], in_=idd[:, :]),
            reads=[], writes=["identF"], tag="ldidF")
    fins = emit_l1_attn_a(ctx, x2, wqT, wkT, wvT, wgT, winT, cosT, sinT, rsg,
                          qT, kT, v, sg, dtl, ident, identF, T=T, D=D,
                          H=H, KVH=KVH)
    ctx.op("sync", lambda e: e.nop(), reads=fins, writes=[], tag="fin")
    ctx.finalize_and_emit()
    return nc


def _b_l1s():
    QH = 4
    nc = bass.Bass("TRN2", target_bir_lowering=False)
    qT = nc.dram_tensor("qT", [QH * 128, L], BF16, kind="ExternalInput")
    kT = nc.dram_tensor("kT", [128, L], BF16, kind="ExternalInput")
    v = nc.dram_tensor("v", [L, 128], BF16, kind="ExternalInput")
    dtl = nc.dram_tensor("dtl", [QH, L], F32, kind="ExternalInput")
    dtb = nc.dram_tensor("dtb", [QH, 1], F32, kind="ExternalInput")
    alogb = nc.dram_tensor("alogb", [QH, 1], F32, kind="ExternalInput")
    idd = nc.dram_tensor("idd", [128, 128], F32, kind="ExternalInput")
    caus = nc.dram_tensor("caus", [128, 128], F32, kind="ExternalInput")
    y = nc.dram_tensor("y", [L, QH * 128], F32, kind="ExternalOutput")
    ctx = Ctx(nc)
    ident = emit_ident(ctx, idd)
    identF = nc.alloc_sbuf_tensor("identF", [128, 128], F32)
    ctx.dma("gpsimd", lambda e: e.dma_start(out=identF[:, :], in_=idd[:, :]),
            reads=[], writes=["identF"], tag="ldidF")
    fins = emit_l1_scan(ctx, qT, kT, v, dtl, dtb, alogb, y, ident, identF,
                        caus, L=L, QH=QH)
    ctx.op("sync", lambda e: e.nop(), reads=fins, writes=[], tag="fin")
    ctx.finalize_and_emit()
    return nc


def _b_l1f():
    E, R = 8, 16
    nc = bass.Bass("TRN2", target_bir_lowering=False)
    x3 = nc.dram_tensor("x3", [T, D], F32, kind="ExternalInput")
    wbT = nc.dram_tensor("wbT", [D, D], F32, kind="ExternalInput")
    wgT = nc.dram_tensor("wgT", [D, E], F32, kind="ExternalInput")
    aT = nc.dram_tensor("aT", [D, E * R], F32, kind="ExternalInput")
    bstk = nc.dram_tensor("bstk", [E * R, D], F32, kind="ExternalInput")
    bb = nc.dram_tensor("bb", [1, D], F32, kind="ExternalInput")
    on = nc.dram_tensor("on", [1, 128], F32, kind="ExternalInput")
    idd = nc.dram_tensor("idd", [128, 128], F32, kind="ExternalInput")
    x4 = nc.dram_tensor("x4", [T, D], F32, kind="ExternalOutput")
    ctx = Ctx(nc)
    ident = emit_ident(ctx, idd)
    fins = emit_l1_ffn(ctx, x3, wbT, wgT, aT, bstk, bb, on, x4, ident,
                       T=T, D=D)
    ctx.op("sync", lambda e: e.nop(), reads=fins, writes=[], tag="fin")
    ctx.finalize_and_emit()
    return nc


def _run(nc, ims, outs):
    res = run_bass_kernel_spmd(nc, ims, core_ids=list(range(NCORE)))
    return [{k: np.asarray(r[k]) for k in outs} for r in res.results]


def kernel(**inputs):
    ins = {k: np.asarray(v) for k, v in inputs.items()}
    idd = np.eye(128, dtype=np.float32)
    caus = np.triu(np.ones((128, 128), np.float32))
    x0 = np.ascontiguousarray(ins["L0_attn_hidden_states"][0], dtype=np.float32)

    gw, gs, gz = (ins["L0_ffn_gate_Wq"], ins["L0_ffn_gate_scale"],
                  ins["L0_ffn_gate_zero"])
    uw, us, uz = (ins["L0_ffn_up_Wq"], ins["L0_ffn_up_scale"],
                  ins["L0_ffn_up_zero"])
    dw = np.pad(ins["L0_ffn_down_Wq"], ((0, 0), (0, 128)))
    ds = np.pad(ins["L0_ffn_down_scale"], ((0, 0), (0, 4)))
    dz = np.pad(ins["L0_ffn_down_zero"], ((0, 0), (0, 4)))
    ims = []
    for c in range(NCORE):
        ims.append({
            "qg": np.ascontiguousarray(gw[:, c * 256:(c + 1) * 256]),
            "sg": np.ascontiguousarray(gs[:, c * 8:(c + 1) * 8]),
            "zg": np.ascontiguousarray(gz[:, c * 8:(c + 1) * 8]),
            "qu": np.ascontiguousarray(uw[:, c * 256:(c + 1) * 256]),
            "su": np.ascontiguousarray(us[:, c * 8:(c + 1) * 8]),
            "zu": np.ascontiguousarray(uz[:, c * 8:(c + 1) * 8]),
            "qd": np.ascontiguousarray(dw[:, c * 704:(c + 1) * 704]),
            "sd": np.ascontiguousarray(ds[:, c * 22:(c + 1) * 22]),
            "zd": np.ascontiguousarray(dz[:, c * 22:(c + 1) * 22]),
            "idd": idd})
    r = _run(_b_dq(), ims, ["og", "ou", "od"])
    pg = np.ascontiguousarray(np.concatenate([x["og"] for x in r], 0))
    pu = np.ascontiguousarray(np.concatenate([x["ou"] for x in r], 0))
    pd = np.ascontiguousarray(np.concatenate([x["od"] for x in r], 0))

    ims = []
    for c in range(NCORE):
        t0 = c * T
        halo = np.zeros((3, D), np.float32) if c == 0 else x0[t0 - 3:t0]
        xh = np.concatenate([halo, x0[t0:t0 + T]], 0)
        ims.append({"xh": xh, "inpw": ins["L0_attn_in_proj_w"],
                    "lnw": ins["L0_attn_ln_w"],
                    "mup": ins["L0_attn_mup_vector"],
                    "convw": ins["L0_attn_conv_w"], "idd": idd})
    r = _run(_b_l0a(), ims, ["sz", "xs", "bT", "cT", "dtr"])
    xs_f = np.concatenate([x["xs"] for x in r], 0)
    sz_f = np.concatenate([x["sz"] for x in r], 0)
    bT_f = np.ascontiguousarray(np.concatenate([x["bT"] for x in r], 1))
    cT_f = np.ascontiguousarray(np.concatenate([x["cT"] for x in r], 1))
    dtr_f = np.concatenate([x["dtr"] for x in r], 1)

    NHH, HD = 8, 64
    ims = []
    for c in range(NCORE):
        hs = slice(c * NHH, (c + 1) * NHH)
        ims.append({"xs": np.ascontiguousarray(
                        xs_f[:, c * NHH * HD:(c + 1) * NHH * HD]),
                    "bT": bT_f, "cT": cT_f,
                    "dtr": np.ascontiguousarray(dtr_f[hs]),
                    "dtb": np.ascontiguousarray(ins["L0_attn_dt_bias"][hs][:, None]),
                    "alog": np.ascontiguousarray(ins["L0_attn_A_log"][hs][:, None]),
                    "drow": np.ascontiguousarray(ins["L0_attn_D"][hs][None, :]),
                    "idd": idd, "caus": caus})
    r = _run(_b_l0s(), ims, ["y"])
    y_f = np.concatenate([x["y"] for x in r], 1)

    foldarr = np.ascontiguousarray(
        ins["L0_attn_norm_w"].reshape(D // 128, 128).T.astype(np.float32))
    wTo = np.ascontiguousarray(ins["L0_attn_out_proj_w"].astype(np.float32))
    ims = []
    for c in range(NCORE):
        t0 = c * T
        ims.append({"y": np.ascontiguousarray(y_f[t0:t0 + T]),
                    "g": np.ascontiguousarray(sz_f[t0:t0 + T]),
                    "wT": wTo, "fold": foldarr,
                    "xres": np.ascontiguousarray(x0[t0:t0 + T]), "idd": idd})
    r = _run(_b_gout(True), ims, ["xout"])
    x1 = [np.ascontiguousarray(x["xout"]) for x in r]

    ims = [{"x1": x1[c], "pg": pg, "pu": pu, "pd": pd, "idd": idd}
           for c in range(NCORE)]
    r = _run(_b_ffn(), ims, ["x2"])
    x2 = [np.ascontiguousarray(x["x2"]) for x in r]

    H, KVH, QH = 32, 8, 4
    wqT = np.ascontiguousarray(ins["L1_attn_Wq"].T.astype(np.float32))
    wkT = np.ascontiguousarray(ins["L1_attn_Wk"].T.astype(np.float32))
    wvT = np.ascontiguousarray(ins["L1_attn_Wv"].T.astype(np.float32))
    wgT = np.ascontiguousarray(ins["L1_attn_Wg"].T.astype(np.float32))
    winT = np.ascontiguousarray(ins["L1_attn_Win"].T.astype(np.float32))
    rsg = _rsign()
    ims = []
    for c in range(NCORE):
        cosT, sinT = _rope_tabs(T, c * T)
        ims.append({"x2": x2[c], "wqT": wqT, "wkT": wkT, "wvT": wvT,
                    "wgT": wgT, "winT": winT, "cosT": cosT, "sinT": sinT,
                    "rsg": rsg, "idd": idd})
    r = _run(_b_l1a(), ims, ["qT", "kT", "v", "sg", "dtl"])
    qT_f = np.concatenate([x["qT"] for x in r], 1)
    kT_f = np.concatenate([x["kT"] for x in r], 1)
    v_f = np.concatenate([x["v"] for x in r], 0)
    sg_f = np.concatenate([x["sg"] for x in r], 0)
    dtl_f = np.concatenate([x["dtl"] for x in r], 1)

    ims = []
    for c in range(NCORE):
        qs = slice(c * QH * 128, (c + 1) * QH * 128)
        hq = slice(c * QH, (c + 1) * QH)
        ims.append({"qT": np.ascontiguousarray(qT_f[qs]),
                    "kT": np.ascontiguousarray(kT_f[c * 128:(c + 1) * 128]),
                    "v": np.ascontiguousarray(v_f[:, c * 128:(c + 1) * 128]),
                    "dtl": np.ascontiguousarray(dtl_f[hq]),
                    "dtb": np.ascontiguousarray(
                        ins["L1_attn_dt_bias"][hq][:, None].astype(np.float32)),
                    "alogb": np.ascontiguousarray(
                        ins["L1_attn_A_log_bias"][hq][:, None]),
                    "idd": idd, "caus": caus})
    r = _run(_b_l1s(), ims, ["y"])
    y1_f = np.concatenate([x["y"] for x in r], 1).astype(np.float32)

    foldg = np.ascontiguousarray(
        np.tile(ins["L1_attn_gnorm_w"][:, None], (1, D // 128)).astype(np.float32))
    wToo = np.ascontiguousarray(ins["L1_attn_Wo"].T.astype(np.float32))
    ims = []
    for c in range(NCORE):
        t0 = c * T
        ims.append({"y": np.ascontiguousarray(y1_f[t0:t0 + T]),
                    "g": np.ascontiguousarray(sg_f[t0:t0 + T]),
                    "wT": wToo, "fold": foldg, "xres": x2[c], "idd": idd})
    r = _run(_b_gout(False), ims, ["xout"])
    x3 = [np.ascontiguousarray(x["xout"]) for x in r]

    E, R = 8, 16
    wbT = np.ascontiguousarray(ins["L1_ffn_W_base"].T.astype(np.float32))
    wgT2 = np.ascontiguousarray(ins["L1_ffn_W_gate"].T.astype(np.float32))
    aT = np.ascontiguousarray(
        ins["L1_ffn_A"].reshape(E * R, D).T.astype(np.float32))
    bstk = np.ascontiguousarray(
        ins["L1_ffn_B"].transpose(0, 2, 1).reshape(E * R, D).astype(np.float32))
    bb = np.ascontiguousarray(ins["L1_ffn_b_base"][None].astype(np.float32))
    on = np.ones((1, 128), np.float32)
    ims = [{"x3": x3[c], "wbT": wbT, "wgT": wgT2, "aT": aT, "bstk": bstk,
            "bb": bb, "on": on, "idd": idd} for c in range(NCORE)]
    r = _run(_b_l1f(), ims, ["x4"])
    out = np.concatenate([x["x4"] for x in r], 0)
    return out[None].astype(np.float32)
